# version 02 (final)
# baseline (speedup 1.0000x reference)
import contextlib
import math
import numpy as np

import concourse.bass as bass
import concourse.mybir as mybir
from concourse.bass_utils import run_bass_kernel_spmd

F32 = mybir.dt.float32
BF16 = mybir.dt.bfloat16
I32 = mybir.dt.int32
AF = mybir.ActivationFunctionType
ALU = mybir.AluOpType
AX = mybir.AxisListType

N_CORES = 8
D = 1024
T = 2048
NDT = D // 128
NTT = T // 128

BARRIER_ENGINES = ("tensor", "scalar", "vector", "gpsimd", "sync")


class Prog:
    def __init__(self, name="prog"):
        self.nc = bass.Bass("TRN2", target_bir_lowering=False)
        self.stack = contextlib.ExitStack()
        self.bar = self.stack.enter_context(self.nc.semaphore("bar"))
        self.chan = {}
        self._drams = {}
        self.scratch_mode = False
        self._scratch_base = 0
        self._scratch_size = 0
        self._scratch_ptr = 0
        self.spill_ok = False
        self._spill_base = 0
        self._spill_size = 0
        self._spill_ptr = 0
        self.phases = []
        self.cur = None
        self._names = {}
        self.phase()

    def name(self, base):
        n = self._names.get(base, 0)
        self._names[base] = n + 1
        return f"{base}_{n}" if n else base

    def sem(self, name):
        return self.stack.enter_context(self.nc.semaphore(self.name(name)))

    def sb(self, name, shape, dtype=F32):
        if self.scratch_mode:
            elems = 1
            for d in shape[1:]:
                elems *= int(d)
            esz = 2 if dtype == BF16 else 4
            off = (self._scratch_ptr + 31) & ~31
            nbytes = elems * esz
            if off + nbytes > self._scratch_size:
                assert self.spill_ok, ("scratch overflow", name, off, nbytes)
                soff = (self._spill_ptr + 31) & ~31
                assert soff + nbytes <= self._spill_size, (
                    "spill overflow", name, soff, nbytes)
                self._spill_ptr = soff + nbytes
                return self.nc.alloc_sbuf_tensor_at(
                    self.name(name), shape, dtype,
                    offset=self._spill_base + soff)
            self._scratch_ptr = off + nbytes
            return self.nc.alloc_sbuf_tensor_at(
                self.name(name), shape, dtype,
                offset=self._scratch_base + off)
        return self.nc.alloc_sbuf_tensor(self.name(name), shape, dtype)

    def scratch_setup(self, nbytes):
        base = (self.nc.sbuf_base + 31) & ~31
        self.nc.alloc_sbuf_tensor("scratch_slab", [128, nbytes // 4], F32)
        self._scratch_base = base
        self._scratch_size = nbytes
        self._scratch_ptr = 0

    def scratch_reset(self):
        self._scratch_ptr = 0
        self._spill_ptr = 0

    def psum(self, name, shape, dtype=F32):
        return self.nc.alloc_psum_tensor(self.name(name), shape, dtype)

    def dram(self, name, shape, dtype=F32, kind="Internal"):
        if name not in self._drams:
            self._drams[name] = self.nc.dram_tensor(name, shape, dtype,
                                                    kind=kind)
        return self._drams[name]

    def phase(self):
        self.cur = {e: {"pre": [], "ops": []} for e in BARRIER_ENGINES}
        self.phases.append(self.cur)

    def op(self, engine, fn):
        self.cur[engine]["ops"].append(fn)

    def t(self, fn): self.op("tensor", fn)
    def s(self, fn): self.op("scalar", fn)
    def v(self, fn): self.op("vector", fn)
    def g(self, fn): self.op("gpsimd", fn)
    def y(self, fn): self.op("sync", fn)

    def _chan(self, name):
        if name not in self.chan:
            self.chan[name] = [self.sem("ch_" + name), 0]
        return self.chan[name]

    def dma(self, dst, src, ch="main", queue="sync", **kw):
        c = self._chan(ch)
        c[1] += 16
        sem, target = c[0], c[1]

        def fn(e):
            e.dma_start(out=dst, in_=src, **kw).then_inc(sem, 16)
            return None
        self.op(queue, fn)
        return (sem, target)

    def wait(self, engine, ticket):
        sem, target = ticket
        self.cur[engine]["pre"].append(lambda e: e.wait_ge(sem, target))

    def wait_all(self, engine, ch):
        c = self._chan(ch)
        sem, target = c[0], c[1]
        self.cur[engine]["pre"].append(lambda e: e.wait_ge(sem, target))

    def finalize(self):
        engs = BARRIER_ENGINES
        n = len(engs)
        bar = self.bar
        phases = self.phases

        def mk(engname):
            def run(eng):
                for i, ph in enumerate(phases):
                    for fn in ph[engname]["pre"]:
                        fn(eng)
                    last = None
                    for fn in ph[engname]["ops"]:
                        last = fn(eng)
                    if last is not None:
                        last.then_inc(bar, 1)
                    else:
                        eng.nop().then_inc(bar, 1)
                    eng.wait_ge(bar, n * (i + 1))
            return run

        with self.nc.Block() as block:
            block.tensor(mk("tensor"))
            block.scalar(mk("scalar"))
            block.vector(mk("vector"))
            block.gpsimd(mk("gpsimd"))
            block.sync(mk("sync"))
        self.stack.close()
        return self.nc



TC = 512
NTC = T // TC


class Ctx:
    def __init__(self, P):
        self.P = P
        nc = P.nc
        self.X = [P.sb(f"x{k}", [128, T]) for k in range(NDT)]
        self._xn = None
        self.pb = [P.psum(f"pb{i}", [128, TC]) for i in range(8)]
        self.ACOLS = 8192
        self.arena = P.sb("arena", [128, self.ACOLS])
        self.ones_col = P.sb("ones_col", [128, 1])
        self.ones_col_b = P.sb("ones_col_b", [128, 1], BF16)
        self.ones_row_b = P.sb("ones_row_b", [1, 128], BF16)
        self.ones_row_f = P.sb("ones_row_f", [1, 128])
        self.ident_b = P.sb("ident_b", [128, 128], BF16)
        self.identf = P.sb("identf", [128, 128])
        self.jmi = P.sb("jmi", [128, 128])
        self.utri_b = P.sb("utri_b", [128, 128], BF16)
        self.ltri_s_b = P.sb("ltri_s_b", [128, 128], BF16)
        self.evict_i = 0
        self._pool = {}

        P.v(lambda v: v.memset(self.ones_col[:], 1.0))
        P.v(lambda v: v.memset(self.ones_col_b[:], 1.0))
        P.v(lambda v: v.memset(self.ones_row_b[:], 1.0))
        P.v(lambda v: v.memset(self.ones_row_f[:], 1.0))
        P.g(lambda g: (g.iota(self.identf[:], pattern=[[1, 128]], base=0,
                              channel_multiplier=-1,
                              allow_small_or_imprecise_dtypes=True), None)[1])
        P.phase()
        P.v(lambda v: v.tensor_copy(self.jmi[:], self.identf[:]))
        P.phase()
        P.v(lambda v: v.tensor_single_scalar(self.identf[:], self.identf[:], 0.0,
                                             ALU.is_equal))
        P.v(lambda v: v.tensor_single_scalar(self.utri_b[:], self.jmi[:], 0.0,
                                             ALU.is_ge))
        P.v(lambda v: v.tensor_single_scalar(self.ltri_s_b[:], self.jmi[:], 0.0,
                                             ALU.is_lt))
        P.phase()
        P.v(lambda v: v.tensor_copy(self.ident_b[:], self.identf[:]))
        P.phase()

    def carve(self, cols_f32):
        assert cols_f32 <= self.ACOLS
        return self.arena[:, 0:cols_f32]

    @property
    def XN(self):
        if self._xn is None:
            sm = self.P.scratch_mode
            self.P.scratch_mode = False
            self._xn = [self.P.sb(f"xn{k}", [128, T], BF16) for k in range(NDT)]
            self.P.scratch_mode = sm
        return self._xn

    def new_frame(self):
        self._pool = {}
        self.P.scratch_reset()

    STATIC_POOLS = ("ln_rows", "attn_wbuf")

    def pool(self, name, shape, dtype=F32):
        key = (name,)
        if name == "attn_wbuf":
            assert not self.P.spill_ok, (
                "frame marked spillable but uses attn_wbuf")
        if name in self.STATIC_POOLS:
            if not hasattr(self, "_static_pool"):
                self._static_pool = {}
            if key not in self._static_pool:
                sm = self.P.scratch_mode
                self.P.scratch_mode = False
                self._static_pool[key] = self.P.sb("pool_" + name, shape, dtype)
                self.P.scratch_mode = sm
            t = self._static_pool[key]
            assert list(t.shape) == list(shape)
            return t
        if key not in self._pool:
            self._pool[key] = self.P.sb("pool_" + name, shape, dtype)
        t = self._pool[key]
        assert list(t.shape) == list(shape)
        return t

    def rows4(self, name, cols=T):
        t1 = self.pool(name + "_a", [128, cols])
        t2 = self.pool(name + "_b", [128, cols])
        return [t1[0:1, :], t1[64:65, :], t2[0:1, :], t2[64:65, :]]

    def evict_engine(self):
        self.evict_i += 1
        return "vector" if (self.evict_i % 2) else "scalar"


def bf16_view(ap):
    return ap.bitcast(BF16)



def host_wT(w):
    O, Din = w.shape
    assert Din % 128 == 0
    return np.ascontiguousarray(w.T.reshape(Din // 128, 128, O).transpose(1, 0, 2))


def load_wT(P, name, O, Din, dtype=BF16, ch="w", queue="gpsimd"):
    kt = Din // 128
    dr = P.dram(name, [128, kt, O], F32, kind="ExternalInput").ap()
    sb = P.sb(name + "_sb", [128, kt, O], dtype)
    tk = P.dma(sb[:], dr, ch=ch, queue=queue)
    return sb, tk


def load_plain(P, name, shape, sbuf_shape=None, dtype=F32, ch="w", queue="sync"):
    dr = P.dram(name, list(shape), F32, kind="ExternalInput").ap()
    sb = P.sb(name + "_sb", list(sbuf_shape or shape), dtype)
    tk = P.dma(sb[:], dr, ch=ch, queue=queue)
    return sb, tk



def proj_fm(P, ctx, wT, O, Din, x_tiles, evict, waits=(), t_lo=0, t_hi=T,
            x_slice=None, kt_list=None):
    kt = Din // 128
    if kt_list is None:
        kt_list = list(range(kt))
    nob = O // 128
    tcs = [(lo, min(lo + TC, t_hi)) for lo in range(t_lo, t_hi, TC)]
    jobs = [(ob, i) for ob in range(nob) for i in range(len(tcs))]
    groups = [jobs[i:i + 4] for i in range(0, len(jobs), 4)]
    pend = None
    for gi, grp in enumerate(groups):
        P.phase()
        if gi == 0:
            for w in waits:
                P.wait("tensor", w)
        bank0 = 4 * (gi % 2)
        for j, (ob, ti) in enumerate(grp):
            pbank = ctx.pb[bank0 + j]
            lo, hi = tcs[ti]
            for ki, k in enumerate(kt_list):
                xs = x_tiles[ki][:, lo:hi] if x_slice is None else x_slice(ki, lo, hi)
                P.t(lambda t, pb=pbank, k=k, ob=ob, xs=xs, st=(ki == 0),
                    sp=(ki == len(kt_list) - 1):
                    t.matmul(pb[:, 0:xs.shape[-1]], wT[:, k, ob * 128:(ob + 1) * 128],
                             xs, start=st, stop=sp))
        if pend:
            for j, (ob, ti) in enumerate(pend[0]):
                lo, hi = tcs[ti]
                evict(ob, lo, hi, ctx.pb[pend[1] + j], ctx.evict_engine())
        pend = (grp, bank0)
    P.phase()
    for j, (ob, ti) in enumerate(pend[0]):
        lo, hi = tcs[ti]
        evict(ob, lo, hi, ctx.pb[pend[1] + j], ctx.evict_engine())
    P.phase()


def evict_copy(P, dst_tiles):
    def ev(ob, lo, hi, psum, eng):
        if eng == "scalar":
            P.s(lambda e: e.copy(dst_tiles[ob][:, lo:hi], psum[:, 0:hi - lo]))
        else:
            P.v(lambda e: e.tensor_copy(dst_tiles[ob][:, lo:hi], psum[:, 0:hi - lo]))
    return ev


def evict_add_into(P, dst_tiles):
    def ev(ob, lo, hi, psum, eng):
        P.v(lambda e: e.tensor_tensor(dst_tiles[ob][:, lo:hi],
                                      dst_tiles[ob][:, lo:hi],
                                      psum[:, 0:hi - lo], ALU.add))
    return ev



def layernorm_fm(P, ctx, x_tiles, xn_tiles, eps=1e-5, d_total=D, f32_tiles=None,
                 stats_only=False):
    ndt = len(x_tiles)
    pb = ctx.pb
    ws = ctx.carve(4096)
    xsqA = ws[:, 0:2048]
    xsqB = ws[:, 2048:4096]
    lnband = ctx.pool("ln_rows", [128, 4096])
    b_r = lnband[0:1, 0:T]
    rstd_r = lnband[0:1, T:2 * T]
    onesr = ctx.ones_row_f

    P.phase()
    for ci in range(NTC):
        for ki in range(ndt):
            P.t(lambda t, ci=ci, ki=ki:
                t.matmul(pb[ci][0:1, :], ctx.ones_col[:],
                         x_tiles[ki][:, ci * TC:(ci + 1) * TC],
                         start=(ki == 0), stop=(ki == ndt - 1)))
    for ki in range(ndt + 1):
        if ki < ndt:
            buf = xsqA if ki % 2 == 0 else xsqB
            if ki % 2 == 0:
                P.s(lambda s, ki=ki, buf=buf: s.activation(buf, x_tiles[ki][:], AF.Square))
            else:
                P.v(lambda v, ki=ki, buf=buf: v.tensor_tensor(buf, x_tiles[ki][:],
                                                              x_tiles[ki][:], ALU.mult))
        if ki > 0:
            buf = xsqA if (ki - 1) % 2 == 0 else xsqB
            for ci in range(NTC):
                P.t(lambda t, ci=ci, ki=ki, buf=buf:
                    t.matmul(pb[4 + ci][0:1, :], ctx.ones_col[:],
                             buf[:, ci * TC:(ci + 1) * TC],
                             start=(ki == 1), stop=(ki == ndt)))
        P.phase()
    inv = 1.0 / d_total
    for ci in range(NTC):
        P.s(lambda s, ci=ci: s.mul(b_r[:, ci * TC:(ci + 1) * TC], pb[ci][0:1, :], inv))
        P.v(lambda v, ci=ci: v.tensor_scalar_mul(rstd_r[:, ci * TC:(ci + 1) * TC],
                                                 pb[4 + ci][0:1, :], inv))
    P.phase()
    for ci in range(NTC):
        P.v(lambda v, ci=ci: v.tensor_tensor(pb[ci][0:1, :],
                                             b_r[:, ci * TC:(ci + 1) * TC],
                                             b_r[:, ci * TC:(ci + 1) * TC], ALU.mult))
    P.phase()
    for ci in range(NTC):
        P.v(lambda v, ci=ci: v.tensor_tensor(rstd_r[:, ci * TC:(ci + 1) * TC],
                                             rstd_r[:, ci * TC:(ci + 1) * TC],
                                             pb[ci][0:1, :], ALU.subtract))
    P.phase()
    P.v(lambda v: v.tensor_scalar_add(rstd_r, rstd_r, eps))
    P.phase()
    P.s(lambda s: s.activation(rstd_r, rstd_r, AF.Sqrt))
    P.phase()
    P.v(lambda v: v.reciprocal(rstd_r, rstd_r))
    P.phase()
    P.v(lambda v: v.scalar_tensor_tensor(b_r, b_r, -1.0, rstd_r, ALU.mult, ALU.mult))
    P.phase()
    if stats_only:
        return
    for ci in range(NTC):
        lo = ci * TC
        ba = pb[(ci % 2) * 2]
        bb = pb[(ci % 2) * 2 + 1]
        P.t(lambda t, lo=lo, ba=ba: t.matmul(ba[:], onesr[:], rstd_r[:, lo:lo + TC],
                                             start=True, stop=True))
        P.t(lambda t, lo=lo, bb=bb: t.matmul(bb[:], onesr[:], b_r[:, lo:lo + TC],
                                             start=True, stop=True))
        P.phase()
        for ki in range(ndt):
            tmp = xsqA[:, ki * 256:ki * 256 + TC] if False else None
            tbuf = (xsqA if ki < 4 else xsqB)[:, (ki % 4) * TC:(ki % 4 + 1) * TC]
            P.v(lambda v, ki=ki, lo=lo, ba=ba, tbuf=tbuf:
                v.tensor_tensor(tbuf, x_tiles[ki][:, lo:lo + TC], ba[:], ALU.mult))
        P.phase()
        for ki in range(ndt):
            tbuf = (xsqA if ki < 4 else xsqB)[:, (ki % 4) * TC:(ki % 4 + 1) * TC]
            P.v(lambda v, ki=ki, lo=lo, bb=bb, tbuf=tbuf:
                v.tensor_tensor(xn_tiles[ki][:, lo:lo + TC], tbuf, bb[:], ALU.add))
            if f32_tiles is not None:
                P.s(lambda s_, ki=ki, lo=lo, bb=bb, tbuf=tbuf:
                    s_.activation(f32_tiles[ki][:, lo:lo + TC], tbuf, AF.Identity)
                    ) if False else None
        P.phase()
    if f32_tiles is not None:
        for ci in range(NTC):
            lo = ci * TC
            ba = pb[(ci % 2) * 2]
            bb = pb[(ci % 2) * 2 + 1]
            P.t(lambda t, lo=lo, ba=ba: t.matmul(ba[:], onesr[:], rstd_r[:, lo:lo + TC],
                                                 start=True, stop=True))
            P.t(lambda t, lo=lo, bb=bb: t.matmul(bb[:], onesr[:], b_r[:, lo:lo + TC],
                                                 start=True, stop=True))
            P.phase()
            for ki in range(ndt):
                tbuf = (xsqA if ki < 4 else xsqB)[:, (ki % 4) * TC:(ki % 4 + 1) * TC]
                P.v(lambda v, ki=ki, lo=lo, ba=ba, tbuf=tbuf:
                    v.tensor_tensor(tbuf, x_tiles[ki][:, lo:lo + TC], ba[:], ALU.mult))
            P.phase()
            for ki in range(ndt):
                tbuf = (xsqA if ki < 4 else xsqB)[:, (ki % 4) * TC:(ki % 4 + 1) * TC]
                P.v(lambda v, ki=ki, lo=lo, bb=bb, tbuf=tbuf:
                    v.tensor_tensor(f32_tiles[ki][:, lo:lo + TC], tbuf, bb[:], ALU.add))
            P.phase()



def build_L0_ffn(P, ctx, tag="L0_ffn"):
    name = tag + "_wT"
    dr = P.dram(name, [128, NDT, D], F32, kind="ExternalInput").ap()
    wf = P.sb(name + "_f", [128, NDT, D])
    wsgn = P.sb(name + "_sgn", [128, NDT, D], BF16)
    tk = P.dma(wf[:], dr, ch="w")
    P.phase()
    P.wait("scalar", tk)
    P.s(lambda s: s.sign(wsgn[:].rearrange("p a b -> p (a b)"),
                         wf[:].rearrange("p a b -> p (a b)")))
    P.phase()
    proj_fm(P, ctx, wsgn, D, D, ctx.XN, evict_add_into(P, ctx.X))


def host_L0_ffn(inputs, tag="L0_ffn"):
    return {tag + "_wT": host_wT(np.asarray(inputs["L0_ffn_weight"]))}



def proj_tok(P, ctx, wT, O, Din, x_tiles, evict, waits=()):
    kt = Din // 128
    ocs = [(lo, min(lo + TC, O)) for lo in range(0, O, TC)]
    jobs = [(ts, oi) for ts in range(NTT) for oi in range(len(ocs))]
    groups = [jobs[i:i + 4] for i in range(0, len(jobs), 4)]
    pend = None
    for gi, grp in enumerate(groups):
        P.phase()
        if gi == 0:
            for w in waits:
                P.wait("tensor", w)
        bank0 = 4 * (gi % 2)
        for j, (ts, oi) in enumerate(grp):
            pbank = ctx.pb[bank0 + j]
            lo, hi = ocs[oi]
            for k in range(kt):
                P.t(lambda t, pb=pbank, k=k, ts=ts, lo=lo, hi=hi, st=(k == 0),
                    sp=(k == kt - 1):
                    t.matmul(pb[:, 0:hi - lo],
                             x_tiles[k][:, ts * 128:(ts + 1) * 128],
                             wT[:, k, lo:hi], start=st, stop=sp))
        if pend:
            for j, (ts, oi) in enumerate(pend[0]):
                lo, hi = ocs[oi]
                evict(ts, lo, hi, ctx.pb[pend[1] + j], ctx.evict_engine())
        pend = (grp, bank0)
    P.phase()
    for j, (ts, oi) in enumerate(pend[0]):
        lo, hi = ocs[oi]
        evict(ts, lo, hi, ctx.pb[pend[1] + j], ctx.evict_engine())
    P.phase()



def attn_core(P, ctx, H, get_qk, Vtok, YT, strip_fn, finish_fn, causal=True,
              with_den=True, hd=64, after_head=None, vcol=None):
    aT2 = ctx.pool("attn_aT", [128, T], BF16)
    aT = [aT2, aT2]
    for h in range(H):
        qf, kf = get_qk(h)
        for ks in range(NTT):
            if causal:
                qlo, qhi = ((ks * 128) // TC) * TC, T
                first_ks_of = lambda qc: 0
                qcs = range((ks * 128) // TC, NTC)
            else:
                qlo, qhi = 0, min(T, ((ks * 128 + 127) // TC + 1) * TC)
                first_ks_of = lambda qc: 4 * qc
                qcs = range(0, min(NTC, (ks * 128 + 127) // TC + 1))
            chunks = [(c, min(c + TC, qhi)) for c in range(qlo, qhi, TC)]
            for ci, (clo, chi) in enumerate(chunks):
                stb = ctx.pb[ci % 2]
                P.phase()
                P.t(lambda t, stb=stb, ks=ks, clo=clo, chi=chi, kf=kf, qf=qf:
                    t.matmul(stb[:, 0:chi - clo],
                             kf[:, ks * 128:(ks + 1) * 128], qf[:, clo:chi],
                             start=True, stop=True))
                if ci > 0:
                    plo, phi = chunks[ci - 1]
                    strip_fn(h, ks, plo, phi, ctx.pb[(ci - 1) % 2], aT[ks % 2], qlo)
            P.phase()
            plo, phi = chunks[-1]
            strip_fn(h, ks, plo, phi, ctx.pb[(len(chunks) - 1) % 2], aT[ks % 2], qlo)
            P.phase()
            for qc in qcs:
                nb = ctx.pb[4 + qc]
                st = (ks == first_ks_of(qc))
                vlo = (h * hd) if vcol is None else vcol(h)
                P.t(lambda t, nb=nb, ks=ks, a=aT[ks % 2], qc=qc, qlo=qlo, st=st,
                    vlo=vlo:
                    t.matmul(nb[0:hd, :], Vtok[ks][:, vlo:vlo + hd],
                             a[:, qc * TC - qlo:qc * TC - qlo + TC],
                             start=st, stop=False, skip_group_check=True))
                if with_den:
                    P.t(lambda t, nb=nb, ks=ks, a=aT[ks % 2], qc=qc, qlo=qlo:
                        t.matmul(nb[hd:hd + 1, :], ctx.ones_col_b[:],
                                 a[:, qc * TC - qlo:qc * TC - qlo + TC],
                                 start=False, stop=False, skip_group_check=True))
        for qc in range(NTC):
            finish_fn(h, qc)
        P.phase()
        if after_head is not None:
            after_head(h)


def host_wT2(w):
    w = np.asarray(w)
    Din, O = w.shape
    return np.ascontiguousarray(w.reshape(Din // 128, 128, O).transpose(1, 0, 2))


def host_col(b):
    b = np.asarray(b)
    return np.ascontiguousarray(b.reshape(-1, 128).T)


def softmax_strip(P, ctx, scale):
    def fn(h, ks, clo, chi, stp, a, qlo):
        vlo = max(clo, ks * 128)
        if vlo < chi:
            P.s(lambda s: s.activation(a[:, vlo - qlo:chi - qlo],
                                       stp[:, vlo - clo:chi - clo], AF.Exp,
                                       scale=scale))
        if clo < vlo:
            P.v(lambda v: v.memset(a[:, clo - qlo:vlo - qlo], 0.0))
        dlo, dhi = max(clo, ks * 128), min(chi, ks * 128 + 128)
        if dlo < dhi:
            P.phase()
            P.v(lambda v: v.tensor_tensor(a[:, dlo - qlo:dhi - qlo],
                                          a[:, dlo - qlo:dhi - qlo],
                                          ctx.utri_b[:, dlo - ks * 128:dhi - ks * 128],
                                          ALU.mult))
    return fn


def softmax_finish(P, ctx, YT, eps=None, hd=64):
    band = ctx.pool("ln_rows", [128, 4096])
    ynum = band[0:hd, 0:TC]
    rr = band[0:1, TC:2 * TC]

    def fn(h, qc):
        td, pbase = (h % 4) // 2, 64 * (h % 2)
        nb = ctx.pb[4 + qc]
        P.phase()
        P.s(lambda s: s.copy(ynum[:], nb[0:hd, :]))
        if eps is not None:
            P.v(lambda v: v.tensor_scalar_add(rr[:], nb[hd:hd + 1, :], eps))
        else:
            P.v(lambda v: v.tensor_copy(rr[:], nb[hd:hd + 1, :]))
        P.phase()
        P.v(lambda v: v.reciprocal(rr[:], rr[:]))
        P.phase()
        P.t(lambda t: t.matmul(ctx.pb[2][0:hd, :], ctx.ones_row_f[:, 0:hd], rr[:],
                               start=True, stop=True))
        P.phase()
        P.v(lambda v: v.tensor_tensor(YT[td][pbase:pbase + hd, qc * TC:(qc + 1) * TC],
                                      ynum[:], ctx.pb[2][0:hd, :], ALU.mult))
        P.phase()
    return fn


def proj_head(P, ctx, wT, olo, x_tiles, dst, bias_col=None, banks=(0, 1)):
    pend = None
    for ci in range(NTC):
        pbk = ctx.pb[banks[ci % 2]]
        P.phase()
        for k in range(NDT):
            P.t(lambda t, pbk=pbk, k=k, ci=ci:
                t.matmul(pbk[0:64, :], wT[:, k, olo:olo + 64],
                         x_tiles[k][:, ci * TC:(ci + 1) * TC],
                         start=(k == 0), stop=(k == NDT - 1)))
        if pend is not None:
            cj, pj = pend
            if bias_col is None:
                P.v(lambda v, cj=cj, pj=pj: v.tensor_copy(
                    dst[:, cj * TC:(cj + 1) * TC], pj[0:64, :]))
            else:
                P.v(lambda v, cj=cj, pj=pj: v.tensor_scalar(
                    dst[:, cj * TC:(cj + 1) * TC], pj[0:64, :], bias_col, None,
                    ALU.add))
        pend = (ci, pbk)
    P.phase()
    cj, pj = pend
    if bias_col is None:
        P.v(lambda v: v.tensor_copy(dst[:, cj * TC:(cj + 1) * TC], pj[0:64, :]))
    else:
        P.v(lambda v: v.tensor_scalar(dst[:, cj * TC:(cj + 1) * TC], pj[0:64, :],
                                      bias_col, None, ALU.add))
    P.phase()


def build_std_attn(P, ctx, tag, with_bqkv=False, with_bo=False, scale=0.125):
    H = 16
    wb = ctx.pool("attn_wbuf", [128, NDT, D], BF16)
    dr_v = P.dram(tag + "_wv", [128, NDT, D], F32, kind="ExternalInput").ap()
    dr_o = P.dram(tag + "_wo", [128, NDT, D], F32, kind="ExternalInput").ap()
    dr_qh = P.dram(tag + "_wqh", [H, 128, NDT, 64], F32, kind="ExternalInput").ap()
    dr_kh = P.dram(tag + "_wkh", [H, 128, NDT, 64], F32, kind="ExternalInput").ap()
    wqh = [ctx.pool("attn_wqh%d" % i, [128, NDT, 64], BF16) for i in range(2)]
    wkh = [ctx.pool("attn_wkh%d" % i, [128, NDT, 64], BF16) for i in range(2)]
    tk_v = P.dma(wb[:], dr_v, ch="w", queue="gpsimd")
    qk_tk = {}
    qk_tk[0] = (P.dma(wqh[0][:], dr_qh[0], ch="w", queue="gpsimd"),
                P.dma(wkh[0][:], dr_kh[0], ch="w", queue="gpsimd"))
    if with_bqkv:
        bq_dr = P.dram(tag + "_bqkv", [128, 3 * NDT], F32, kind="ExternalInput").ap()
        bq_sb = P.sb(tag + "_bqkv_sb", [128, 3 * NDT])
        P.dma(bq_sb[:], bq_dr, ch="wsmall")
        bv_dr = P.dram(tag + "_bvrow", [1, D], F32, kind="ExternalInput").ap()
        bv_row = P.sb(tag + "_bvrow_sb", [1, D], BF16)
        P.dma(bv_row[:], bv_dr, ch="wsmall", queue="gpsimd")
    if with_bo:
        bo_dr = P.dram(tag + "_bo", [128, NDT], F32, kind="ExternalInput").ap()
        bo_sb = P.sb(tag + "_bo_sb", [128, NDT])
        P.dma(bo_sb[:], bo_dr, ch="wsmall")
    P.phase()
    if with_bqkv or with_bo:
        for e in ("vector", "scalar", "tensor"):
            P.wait_all(e, "wsmall")
    P.phase()

    arena_b = ctx.arena[:].bitcast(BF16)
    Vtok = [arena_b[:, ts * D:(ts + 1) * D] for ts in range(NTT)]
    yth = ctx.pool("attn_yth", [128, 2, T], BF16)
    YT = [yth[:, k, :] for k in range(2)]
    qkband = ctx.pool("attn_qkh", [64, 2 * T], BF16)
    QH = qkband[:, 0:T]
    KH = qkband[:, T:2 * T]

    def evv(ts, lo, hi, psum, eng):
        P.v(lambda v: v.tensor_copy(Vtok[ts][:, lo:hi], psum[:, 0:hi - lo]))

    if with_bqkv:
        def x_slice_v(ki, lo, hi):
            return None
        kt = NDT
        jobs = [(ts, oc) for ts in range(NTT) for oc in range(0, D, TC)]
        groups = [jobs[i:i + 4] for i in range(0, len(jobs), 4)]
        pend = None
        for gi, grp in enumerate(groups):
            P.phase()
            if gi == 0:
                P.wait("tensor", tk_v)
            bank0 = 4 * (gi % 2)
            for j, (ts, lo) in enumerate(grp):
                pbank = ctx.pb[bank0 + j]
                for k in range(kt):
                    P.t(lambda t, pb=pbank, k=k, ts=ts, lo=lo:
                        t.matmul(pb[:], ctx.XN[k][:, ts * 128:(ts + 1) * 128],
                                 wb[:, k, lo:lo + TC], start=(k == 0), stop=False))
                P.t(lambda t, pb=pbank, ts=ts, lo=lo:
                    t.matmul(pb[:], ctx.ones_row_b[:],
                             bv_row[:, lo:lo + TC], start=False, stop=True))
            if pend:
                for j, (ts, lo) in enumerate(pend[0]):
                    evv(ts, lo, lo + TC, ctx.pb[pend[1] + j], None)
            pend = (grp, bank0)
        P.phase()
        for j, (ts, lo) in enumerate(pend[0]):
            evv(ts, lo, lo + TC, ctx.pb[pend[1] + j], None)
        P.phase()
    else:
        proj_tok(P, ctx, wb, D, D, ctx.XN, evv, waits=[tk_v])
    P.phase()
    tk_o = P.dma(wb[:], dr_o, ch="w", queue="gpsimd")
    P.phase()

    def get_qk(h):
        ob, pb64 = h // 2, 64 * (h % 2)
        bq = bk = None
        if with_bqkv:
            bq = bq_sb[pb64:pb64 + 64, 0 * NDT + ob:0 * NDT + ob + 1]
            bk = bq_sb[pb64:pb64 + 64, 1 * NDT + ob:1 * NDT + ob + 1]
        if h + 1 < H:
            qk_tk[h + 1] = (
                P.dma(wqh[(h + 1) % 2][:], dr_qh[h + 1], ch="w", queue="gpsimd"),
                P.dma(wkh[(h + 1) % 2][:], dr_kh[h + 1], ch="w", queue="gpsimd"))
        P.phase()
        P.wait("tensor", qk_tk[h][0])
        P.wait("tensor", qk_tk[h][1])
        P.phase()
        proj_head(P, ctx, wqh[h % 2], 0, ctx.XN, QH, bias_col=bq, banks=(0, 1))
        proj_head(P, ctx, wkh[h % 2], 0, ctx.XN, KH, bias_col=bk, banks=(0, 1))
        return QH, KH

    def evo_half(half):
        def evo(ob, lo, hi, psum, eng):
            if with_bo and half == 0:
                P.v(lambda v: v.scalar_tensor_tensor(ctx.X[ob][:, lo:hi],
                                                     psum[:, 0:hi - lo],
                                                     bo_sb[:, ob:ob + 1],
                                                     ctx.X[ob][:, lo:hi],
                                                     ALU.add, ALU.add))
            else:
                P.v(lambda v: v.tensor_tensor(ctx.X[ob][:, lo:hi],
                                              ctx.X[ob][:, lo:hi],
                                              psum[:, 0:hi - lo], ALU.add))
        return evo

    def after_head(h):
        if h % 4 == 3:
            q = h // 4
            proj_fm(P, ctx, wb, D, D, YT, evo_half(q),
                    waits=[tk_o] if q == 0 else (), kt_list=[2 * q, 2 * q + 1])

    attn_core(P, ctx, H, get_qk, Vtok, YT, softmax_strip(P, ctx, scale),
              softmax_finish(P, ctx, YT), causal=True, with_den=True,
              after_head=after_head)


def build_L1_attn_a(P, ctx):
    build_std_attn(P, ctx, "L1aa", with_bo=True, scale=0.125)


def host_wT2_heads(w, H=16):
    a = host_wT2(w)
    O = a.shape[2]
    hd = O // H
    return np.ascontiguousarray(
        a.reshape(128, a.shape[1], H, hd).transpose(2, 0, 1, 3))


def host_L1_attn_a(inputs):
    return {
        "L1aa_wqh": host_wT2_heads(inputs["L1_attn_a_Wq"]),
        "L1aa_wkh": host_wT2_heads(inputs["L1_attn_a_Wk"]),
        "L1aa_wv": host_wT2(inputs["L1_attn_a_Wv"]),
        "L1aa_wo": host_wT2(inputs["L1_attn_a_Wo"]),
        "L1aa_bo": host_col(inputs["L1_attn_a_bo"]),
    }


def build_L2_attn_b(P, ctx):
    build_std_attn(P, ctx, "L2ab", with_bqkv=True, with_bo=True, scale=0.125)


def host_L2_attn_b(inputs):
    wqkv = np.asarray(inputs["L2_attn_b_W_qkv"])
    b = np.asarray(inputs["L2_attn_b_b_qkv"])
    return {
        "L2ab_wqh": host_wT2_heads(wqkv[:, 0:D]),
        "L2ab_wkh": host_wT2_heads(wqkv[:, D:2 * D]),
        "L2ab_wv": host_wT2(wqkv[:, 2 * D:3 * D]),
        "L2ab_wo": host_wT2(np.asarray(inputs["L2_attn_b_W_out"])),
        "L2ab_bqkv": np.concatenate([host_col(b[0:D]), host_col(b[D:2 * D]),
                                     host_col(b[2 * D:3 * D])], axis=1),
        "L2ab_bvrow": np.ascontiguousarray(b[2 * D:3 * D][None, :]),
        "L2ab_bo": host_col(np.asarray(inputs["L2_attn_b_b_out"])),
    }


def build_L3_attn_b(P, ctx):
    tag = "L3ab"
    H = 16
    SCALE = 1.0 / 8.0
    band = ctx.pool("ln_rows", [128, 4096])
    b_r = band[0:1, 0:T]
    rstd_r = band[0:1, T:2 * T]
    wb = ctx.pool("attn_wbuf", [128, NDT, D], BF16)
    dr_v = P.dram(tag + "_wv", [128, NDT, D], F32, kind="ExternalInput").ap()
    dr_o = P.dram(tag + "_wo", [128, NDT, D], F32, kind="ExternalInput").ap()
    dr_qh = P.dram(tag + "_wqh", [H, 128, NDT, 64], F32, kind="ExternalInput").ap()
    dr_kh = P.dram(tag + "_wkh", [H, 128, NDT, 64], F32, kind="ExternalInput").ap()
    wqh = [ctx.pool("l3_wqh%d" % i, [128, NDT, 64]) for i in range(2)]
    wkh = [ctx.pool("l3_wkh%d" % i, [128, NDT, 64]) for i in range(2)]
    tk_v = P.dma(wb[:], dr_v, ch="w", queue="gpsimd")
    qk_tk = {0: (P.dma(wqh[0][:], dr_qh[0], ch="w"),
                 P.dma(wkh[0][:], dr_kh[0], ch="w"))}
    g_dr = P.dram(tag + "_gamma", [1, H], F32, kind="ExternalInput").ap()
    g_row = P.sb(tag + "_g_sb", [1, H])
    P.dma(g_row[:], g_dr, ch="wsmall")
    gbc = P.sb(tag + "_gbc", [128, H])
    kncol = P.sb(tag + "_kncol", [128, 2])
    tmpc = ctx.pool("attn_tmp", [128, TC])
    l3r = ctx.pool("l3_rows", [128, 2048], BF16)
    qn_r = l3r[32:33, 0:T]
    kn_r = l3r[64:65, 0:T]
    o33 = P.sb(tag + "_ones32", [33, 128], BF16)
    wsum_row = P.sb(tag + "_wsum", [1, 128])
    rstd_bc = ctx.pool("l3_rstd_bc", [128, T], BF16)

    P.phase()
    for e in ("vector", "scalar", "tensor"):
        P.wait_all(e, "wsmall")
    P.phase()
    P.v(lambda v: v.tensor_scalar_mul(g_row[:], g_row[:], -SCALE))
    P.v(lambda v: v.memset(o33[:], 1.0))
    P.phase()
    P.t(lambda t: t.matmul(ctx.pb[0][:, 0:H], ctx.ones_row_f[:], g_row[:],
                           start=True, stop=True))
    for ci in range(NTC):
        P.t(lambda t, ci=ci: t.matmul(ctx.pb[4 + ci][:], ctx.ones_row_f[:],
                                      rstd_r[:, ci * TC:(ci + 1) * TC],
                                      start=True, stop=True))
    P.phase()
    P.v(lambda v: v.tensor_copy(gbc[:], ctx.pb[0][:, 0:H]))
    for ci in range(NTC):
        P.s(lambda s, ci=ci: s.copy(rstd_bc[:, ci * TC:(ci + 1) * TC],
                                    ctx.pb[4 + ci][:]))
    P.phase()

    arena_b = ctx.arena[:].bitcast(BF16)
    Vtok = [arena_b[:, ts * D:(ts + 1) * D] for ts in range(NTT)]
    yth = ctx.pool("attn_yth", [128, 2, T], BF16)
    YT = [yth[:, k, :] for k in range(2)]
    qkband = ctx.pool("l3_qkh", [64, 2 * T], BF16)
    QH = qkband[:, 0:T]
    KH = qkband[:, T:2 * T]

    xbts = [ctx.pool("l3_xbts%d" % i, [128, NDT, 128], BF16) for i in range(2)]

    def prep_ts(ts):
        bb = ctx.pb[4 * (ts % 2) + 2]
        P.t(lambda t: t.matmul(bb[:, 0:128], ctx.ones_row_f[:],
                               b_r[:, ts * 128:(ts + 1) * 128], start=True,
                               stop=True))
        P.phase()
        for half in range(2):
            for k in range(4 * half, 4 * half + 4):
                P.v(lambda v, k=k: v.tensor_tensor(
                    tmpc[:, (k % 4) * 128:(k % 4) * 128 + 128],
                    ctx.X[k][:, ts * 128:(ts + 1) * 128],
                    rstd_bc[:, ts * 128:(ts + 1) * 128], ALU.mult))
            P.phase()
            for k in range(4 * half, 4 * half + 4):
                P.v(lambda v, k=k: v.tensor_tensor(
                    xbts[ts % 2][:, k, :], tmpc[:, (k % 4) * 128:(k % 4) * 128 + 128],
                    bb[:, 0:128], ALU.add))
            P.phase()

    prep_ts(0)
    P.phase()
    P.wait("tensor", tk_v)
    P.phase()
    pendv = None
    for ts in range(NTT):
        bank0 = 4 * (ts % 2)
        cur = xbts[ts % 2]
        if ts + 1 < NTT:
            prep_ts(ts + 1)
        for j, lo in enumerate((0, TC)):
            pbank = ctx.pb[bank0 + j]
            for k in range(NDT):
                P.t(lambda t, pb=pbank, k=k, lo=lo, cur=cur:
                    t.matmul(pb[:], cur[:, k, :], wb[:, k, lo:lo + TC],
                             start=(k == 0), stop=(k == NDT - 1)))
        if pendv is not None:
            tsp, b0p = pendv
            for j, lo in enumerate((0, TC)):
                P.v(lambda v, ts=tsp, lo=lo, pj=ctx.pb[b0p + j]:
                    v.tensor_copy(Vtok[ts][:, lo:lo + TC], pj[:]))
        pendv = (ts, bank0)
        P.phase()
    tsp, b0p = pendv
    for j, lo in enumerate((0, TC)):
        P.v(lambda v, ts=tsp, lo=lo, pj=ctx.pb[b0p + j]:
            v.tensor_copy(Vtok[ts][:, lo:lo + TC], pj[:]))
    P.phase()
    tk_o = P.dma(wb[:], dr_o, ch="w", queue="gpsimd")
    P.phase()

    def get_qk(h):
        if h + 1 < H:
            qk_tk[h + 1] = (P.dma(wqh[(h + 1) % 2][:], dr_qh[h + 1], ch="w"),
                            P.dma(wkh[(h + 1) % 2][:], dr_kh[h + 1], ch="w"))
        P.phase()
        P.wait("tensor", qk_tk[h][0])
        P.wait("tensor", qk_tk[h][1])
        P.phase()
        for k in range(NDT):
            P.t(lambda t, k=k: t.matmul(ctx.pb[0][0:1, 0:64], ctx.ones_col[:],
                                        wqh[h % 2][:, k, :], start=(k == 0),
                                        stop=(k == NDT - 1), skip_group_check=True))
            P.t(lambda t, k=k: t.matmul(ctx.pb[1][0:1, 0:64], ctx.ones_col[:],
                                        wkh[h % 2][:, k, :], start=(k == 0),
                                        stop=(k == NDT - 1), skip_group_check=True))
        P.phase()
        P.v(lambda v: v.tensor_copy(wsum_row[:, 0:64], ctx.pb[0][0:1, 0:64]))
        P.v(lambda v: v.tensor_copy(wsum_row[:, 64:128], ctx.pb[1][0:1, 0:64]))
        P.phase()
        for (wgt, dst, ws_lo) in ((wqh[h % 2], QH, 0), (wkh[h % 2], KH, 64)):
            pend2 = None
            for ci in range(NTC):
                pbk = ctx.pb[2 + ci % 2]
                P.phase()
                for k in range(NDT):
                    P.t(lambda t, pbk=pbk, k=k, ci=ci, wgt=wgt:
                        t.matmul(pbk[0:64, :], wgt[:, k, :],
                                 ctx.X[k][:, ci * TC:(ci + 1) * TC],
                                 start=(k == 0), stop=False))
                P.t(lambda t, pbk=pbk, ci=ci, ws_lo=ws_lo:
                    t.matmul(pbk[0:64, :], wsum_row[:, ws_lo:ws_lo + 64],
                             b_r[0:1, ci * TC:(ci + 1) * TC], start=False, stop=True))
                if pend2 is not None:
                    cj, pj = pend2
                    P.v(lambda v, cj=cj, pj=pj, dst=dst:
                        v.tensor_tensor(dst[:, cj * TC:(cj + 1) * TC], pj[0:64, :],
                                        rstd_bc[0:64, cj * TC:(cj + 1) * TC],
                                        ALU.mult))
                pend2 = (ci, pbk)
            P.phase()
            cj, pj = pend2
            P.v(lambda v, cj=cj, pj=pj, dst=dst:
                v.tensor_tensor(dst[:, cj * TC:(cj + 1) * TC], pj[0:64, :],
                                rstd_bc[0:64, cj * TC:(cj + 1) * TC], ALU.mult))
            P.phase()
        for (src, dst) in ((QH, qn_r), (KH, kn_r)):
            for ci in range(NTC):
                P.s(lambda s, src=src, ci=ci:
                    s.activation(tmpc[0:64, :], src[:, ci * TC:(ci + 1) * TC],
                                 AF.Square))
                P.phase()
                P.t(lambda t, ci=ci:
                    t.matmul(ctx.pb[2][0:1, :], ctx.ones_col[0:64, :], tmpc[0:64, :],
                             start=True, stop=True))
                P.phase()
                P.v(lambda v, dst=dst, ci=ci:
                    v.tensor_copy(dst[:, ci * TC:(ci + 1) * TC], ctx.pb[2][0:1, :]))
            P.phase()
        return QH, KH

    def strip_fn(h, ks, clo, chi, stp, a, qlo):
        w = chi - clo
        P.t(lambda t: t.matmul(ctx.pb[2][:, 0:1], kn_r[:, ks * 128:(ks + 1) * 128],
                               ctx.ones_col_b[64:65, 0:1], start=True, stop=True))
        P.t(lambda t: t.matmul(ctx.pb[3][:, 0:w], o33[32:33, :],
                               qn_r[:, clo:chi], start=True, stop=True))
        P.phase()
        P.v(lambda v: v.tensor_copy(kncol[:, 0:1], ctx.pb[2][:, 0:1]))
        P.phase()
        P.v(lambda v: v.tensor_scalar(tmpc[:, 0:w], stp[:, 0:w], -2.0,
                                      kncol[:, 0:1], ALU.mult, ALU.add))
        P.phase()
        P.v(lambda v: v.tensor_tensor(tmpc[:, 0:w], tmpc[:, 0:w],
                                      ctx.pb[3][:, 0:w], ALU.add))
        P.phase()
        P.v(lambda v: v.tensor_scalar_max(tmpc[:, 0:w], tmpc[:, 0:w], 0.0))
        P.phase()
        P.s(lambda s: s.activation(a[:, clo - qlo:chi - qlo], tmpc[:, 0:w],
                                   AF.Exp, scale=gbc[:, h:h + 1]))
        P.phase()
        dlo, dhi = max(clo, ks * 128), min(chi, ks * 128 + 128)
        if dlo < dhi:
            P.v(lambda v: v.tensor_tensor(a[:, dlo - qlo:dhi - qlo],
                                          a[:, dlo - qlo:dhi - qlo],
                                          ctx.ltri_s_b[:, dlo - ks * 128:dhi - ks * 128],
                                          ALU.mult))
        tlo = max(clo, ks * 128 + 128)
        if tlo < chi:
            P.v(lambda v: v.memset(a[:, tlo - qlo:chi - qlo], 0.0))

    def finish_fn(h, qc):
        td, pbase = (h % 4) // 2, 64 * (h % 2)
        nb = ctx.pb[4 + qc]
        P.phase()
        P.s(lambda s: s.copy(YT[td][pbase:pbase + 64, qc * TC:(qc + 1) * TC],
                             nb[0:64, :]))
        P.phase()

    def evo_half(half):
        def evo(ob, lo, hi, psum, eng):
            P.v(lambda v: v.tensor_tensor(ctx.X[ob][:, lo:hi], ctx.X[ob][:, lo:hi],
                                          psum[:, 0:hi - lo], ALU.add))
        return evo

    def after_head(h):
        if h % 4 == 3:
            q = h // 4
            proj_fm(P, ctx, wb, D, D, YT, evo_half(q),
                    waits=[tk_o] if q == 0 else (), kt_list=[2 * q, 2 * q + 1])

    attn_core(P, ctx, H, get_qk, Vtok, YT, strip_fn, finish_fn, causal=False,
              with_den=False, after_head=after_head)



def host_L3_attn_b(inputs):
    return {
        "L3ab_wqh": host_wT2_heads(inputs["L3_attn_b_Wq"]),
        "L3ab_wkh": host_wT2_heads(inputs["L3_attn_b_Wk"]),
        "L3ab_wv": host_wT2(inputs["L3_attn_b_Wv"]),
        "L3ab_wo": host_wT2(inputs["L3_attn_b_Wo"]),
        "L3ab_gamma": np.ascontiguousarray(np.asarray(inputs["L3_attn_b_gamma"])[None, :]),
    }



def moe_gating(P, ctx, tag, E, gwT, gb_bc, l2_quirk, cwcols, tk_gw):
    idx8 = ctx.pool("moe_idx8", [128, 8])
    gbuf = ctx.pool("moe_gbuf", [128, 24])
    P.phase()
    P.g(lambda g: (g.iota(idx8[:], pattern=[[1, 8]], base=0, channel_multiplier=0,
                          allow_small_or_imprecise_dtypes=True), None)[1])
    P.phase()
    for ts in range(NTT):
        pbk = ctx.pb[ts % 2]
        P.phase()
        if ts == 0:
            P.wait("tensor", tk_gw)
        for k in range(NDT):
            P.t(lambda t, pbk=pbk, k=k, ts=ts:
                t.matmul(pbk[:, 0:E], ctx.XN[k][:, ts * 128:(ts + 1) * 128],
                         gwT[:, k, 0:E], start=(k == 0), stop=(k == NDT - 1)))
        P.phase()
        probs = gbuf[:, 0:8]
        mcol = gbuf[:, 8:9]
        scr = gbuf[:, 9:17]
        i1 = gbuf[:, 17:18]
        i2 = gbuf[:, 18:19]
        m1 = gbuf[:, 20:21]
        m2 = gbuf[:, 21:22]
        if gb_bc is not None:
            P.v(lambda v, pbk=pbk: v.tensor_tensor(probs, pbk[:, 0:E], gb_bc,
                                                   ALU.add))
        else:
            P.v(lambda v, pbk=pbk: v.tensor_copy(probs, pbk[:, 0:E]))
        P.phase()
        P.v(lambda v: v.reduce_max(mcol, probs, axis=AX.X))
        P.phase()
        P.v(lambda v: v.tensor_scalar_mul(mcol, mcol, -1.0))
        P.phase()
        P.s(lambda s: s.activation(probs, probs, AF.Exp, bias=mcol))
        P.phase()
        P.v(lambda v: v.reduce_sum(mcol, probs, axis=AX.X))
        P.phase()
        P.v(lambda v: v.reciprocal(mcol, mcol))
        P.phase()
        P.v(lambda v: v.tensor_scalar(probs, probs, mcol, None, ALU.mult))
        P.phase()
        P.v(lambda v: v.reduce_max(m1, probs, axis=AX.X))
        P.phase()
        P.v(lambda v: v.tensor_scalar(scr, probs, m1, None, ALU.is_equal))
        P.phase()
        P.v(lambda v: v.tensor_scalar(scr, scr, -8.0, 8.0, ALU.mult, ALU.add))
        P.phase()
        P.v(lambda v: v.tensor_tensor(scr, scr, idx8[:], ALU.add))
        P.phase()
        P.v(lambda v: v.tensor_reduce(i1, scr, AX.X, ALU.min))
        P.phase()
        P.v(lambda v: v.tensor_scalar(scr, idx8[:], i1, None, ALU.is_equal))
        P.phase()
        P.v(lambda v: v.scalar_tensor_tensor(scr, scr, -2.0, probs, ALU.mult,
                                             ALU.add))
        P.phase()
        P.v(lambda v: v.reduce_max(m2, scr, axis=AX.X))
        P.phase()
        P.v(lambda v: v.tensor_scalar(scr, scr, m2, None, ALU.is_equal))
        P.phase()
        P.v(lambda v: v.tensor_scalar(scr, scr, -8.0, 8.0, ALU.mult, ALU.add))
        P.phase()
        P.v(lambda v: v.tensor_tensor(scr, scr, idx8[:], ALU.add))
        P.phase()
        P.v(lambda v: v.tensor_reduce(i2, scr, AX.X, ALU.min))
        P.phase()
        if l2_quirk:
            c1, c2 = probs[:, 0:1], probs[:, 1:2]
        else:
            c1, c2 = m1, m2
        P.v(lambda v: v.tensor_scalar(scr, idx8[:], i1, None, ALU.is_equal))
        P.phase()
        P.v(lambda v, c1=c1: v.tensor_scalar(scr, scr, c1, None, ALU.mult))
        P.phase()
        P.v(lambda v, ts=ts: v.tensor_scalar(cwcols[:, ts, :], idx8[:], i2, None,
                                             ALU.is_equal))
        P.phase()
        P.v(lambda v, c2=c2, ts=ts: v.tensor_scalar(cwcols[:, ts, :],
                                                    cwcols[:, ts, :], c2, None,
                                                    ALU.mult))
        P.phase()
        P.v(lambda v, ts=ts: v.tensor_tensor(cwcols[:, ts, :], cwcols[:, ts, :],
                                             scr, ALU.add))
        P.phase()


def moe_cw_row(P, ctx, cwcols, e, cw_row, cw_bc):
    for ts in range(NTT):
        P.t(lambda t, ts=ts: t.matmul(
            ctx.pb[4 + (ts // 4) % 2][0:1, (ts % 4) * 128:(ts % 4) * 128 + 128],
            cwcols[:, ts, e:e + 1], ctx.identf[:], start=True, stop=True,
            skip_group_check=True))
        if ts % 4 == 3:
            P.phase()
            P.v(lambda v, ts=ts: v.tensor_copy(
                cw_row[:, (ts - 3) * 128:(ts + 1) * 128],
                ctx.pb[4 + (ts // 4) % 2][0:1, :]))
            P.phase()
    P.phase()
    for ci in range(NTC):
        P.t(lambda t, ci=ci: t.matmul(ctx.pb[4 + ci % 2][:], ctx.ones_row_f[:],
                                      cw_row[:, ci * TC:(ci + 1) * TC],
                                      start=True, stop=True))
        P.phase()
        P.v(lambda v, ci=ci: v.tensor_copy(cw_bc[:, ci * TC:(ci + 1) * TC],
                                           ctx.pb[4 + ci % 2][:]))
        P.phase()
    P.phase()


def proj_sliced(P, ctx, wslice_of, load_slice, x_tiles, evict, extra_mm=None,
                first_wait=None):
    tk = {0: load_slice(0)}
    pend = None
    gi = 0
    for ob in range(NDT):
        for ci2 in range(0, NTC, 2):
            P.phase()
            if ci2 == 0:
                P.wait("tensor", tk[ob])
                if ob == 0 and first_wait is not None:
                    P.wait("tensor", first_wait)
                if ob + 1 < NDT:
                    tk[ob + 1] = load_slice(ob + 1)
            bank0 = 4 * (gi % 2)
            grp = [(ob, ci2), (ob, ci2 + 1)]
            for j, (o_, ci) in enumerate(grp):
                pbank = ctx.pb[bank0 + j]
                lo = ci * TC
                for k in range(NDT):
                    P.t(lambda t, pb=pbank, k=k, o_=o_, lo=lo:
                        t.matmul(pb[:], wslice_of(o_)[:, k, :],
                                 x_tiles[k][:, lo:lo + TC], start=(k == 0),
                                 stop=(k == NDT - 1 and extra_mm is None)))
                if extra_mm is not None:
                    extra_mm(pbank, o_, lo, lo + TC)
            if pend is not None:
                for j, (o_, ci) in enumerate(pend[0]):
                    evict(o_, ci * TC, (ci + 1) * TC, ctx.pb[pend[1] + j], None)
            pend = (grp, bank0)
            gi += 1
    P.phase()
    for j, (o_, ci) in enumerate(pend[0]):
        evict(o_, ci * TC, (ci + 1) * TC, ctx.pb[pend[1] + j], None)
    P.phase()


def build_moe(P, ctx, tag, kind):
    E = 8
    gw_dr = P.dram(tag + "_gw", [128, NDT, 8], F32, kind="ExternalInput").ap()
    gw_sb = P.sb(tag + "_gw_sb", [128, NDT, 8], BF16)
    tk_gw = P.dma(gw_sb[:], gw_dr, ch="w", queue="gpsimd")
    cwcols = ctx.pool("moe_cwcols", [128, NTT, 8])
    band = ctx.pool("ln_rows", [128, 4096])
    cw_bc = band[:, 0:T]
    cw_row = band[0:1, T:2 * T]
    bufA = [ctx.arena[:].bitcast(BF16)[:, k * T:(k + 1) * T] for k in range(NDT)]
    bufB_t = ctx.pool("moe_bufB", [128, NDT, T], BF16)
    bufB = [bufB_t[:, k, :] for k in range(NDT)]
    tmpe = ctx.pool("attn_tmp", [128, TC])
    gbbc = None
    if kind == "l2":
        gbrow_dr = P.dram(tag + "_gb", [1, 8], F32, kind="ExternalInput").ap()
        gbrow = P.sb(tag + "_gb_sb", [1, 8])
        P.dma(gbrow[:], gbrow_dr, ch="wsmall")
        brow_dr = P.dram(tag + "_brows", [E, 1, 3 * D], F32,
                         kind="ExternalInput").ap()
        brow_one = P.sb(tag + "_brow", [1, 3 * D])
        brow_sb = [brow_one, brow_one]
        gbbc_t = ctx.pool("moe_gbbc", [128, 8])
        P.phase()
        for e_ in ("vector", "scalar", "tensor"):
            P.wait_all(e_, "wsmall")
        P.phase()
        P.t(lambda t: t.matmul(ctx.pb[0][:, 0:8], ctx.ones_row_f[:], gbrow[:],
                               start=True, stop=True))
        P.phase()
        P.v(lambda v: v.tensor_copy(gbbc_t[:], ctx.pb[0][:, 0:8]))
        P.phase()
        gbbc = gbbc_t[:, 0:8]
    moe_gating(P, ctx, tag, E, gw_sb, gbbc, kind == "l2", cwcols, tk_gw)

    NE = E + (1 if kind == "l3" else 0)
    NM = 3 if kind == "l2" else 2
    if kind == "l2":
        wdr = [P.dram(tag + "_w%d" % m, [E, NDT, 128, NDT, 128], F32,
                      kind="ExternalInput").ap() for m in range(3)]
        wsl = [ctx.pool("moe_wsl%d" % i, [128, NDT, 128], BF16) for i in range(2)]
    else:
        wdr13 = P.dram(tag + "_w13", [NE, NDT, 128, NDT, 256], F32,
                       kind="ExternalInput").ap()
        wdr2 = P.dram(tag + "_w2", [NE, NDT, 128, NDT, 128], F32,
                      kind="ExternalInput").ap()
        wsl13 = [ctx.pool("moe_wsl13_%d" % i, [128, NDT, 256], BF16)
                 for i in range(2)]
        wsl = [ctx.pool("moe_wsl%d" % i, [128, NDT, 128], BF16) for i in range(2)]

    for e in range(NE):
        shared = (kind == "l3" and e == E)
        if not shared:
            moe_cw_row(P, ctx, cwcols, e, cw_row, cw_bc)
            P.phase()
            for k in range(NDT):
                P.v(lambda v, k=k: v.tensor_tensor(bufA[k][:], ctx.XN[k][:],
                                                   cw_bc[:], ALU.mult))
            P.phase()
            xin = bufA
        else:
            xin = ctx.XN

        if kind == "l2":
            tkb = P.dma(brow_sb[e % 2][:], brow_dr[e], ch="wsmall")
            P.phase()
            P.wait("tensor", tkb)
            P.phase()

            def bias_mm(m, e=e):
                def fn(pbank, ob, lo, hi):
                    P.t(lambda t, pbank=pbank, ob=ob, lo=lo, hi=hi, m=m, e=e:
                        t.matmul(pbank[:, 0:hi - lo],
                                 brow_sb[e % 2][:, m * D + ob * 128:
                                                m * D + (ob + 1) * 128],
                                 cw_row[:, lo:hi], start=False, stop=True))
                return fn

            def ev_relu(dst):
                def ev(ob, lo, hi, psum, eng):
                    P.v(lambda v: v.tensor_scalar(dst[ob][:, lo:hi],
                                                  psum[:, 0:hi - lo], 0.0, None,
                                                  ALU.max))
                return ev

            for m, (src, dst_ev) in enumerate((
                    (bufA, ev_relu(bufB)), (bufB, ev_relu(bufA)),
                    (bufA, evict_add_into(P, ctx.X)))):
                def load_slice(ob, e=e, m=m):
                    return P.dma(wsl[ob % 2][:], wdr[m][e, ob], ch="w",
                                 queue="gpsimd")
                proj_sliced(P, ctx, lambda ob: wsl[ob % 2], load_slice, src,
                            dst_ev, extra_mm=bias_mm(m))
        else:
            def load13(ob, e=e):
                return P.dma(wsl13[ob % 2][:], wdr13[e, ob], ch="w",
                             queue="gpsimd")
            tk13 = {0: load13(0)}
            pend = None
            gi = 0
            for ob in range(NDT):
                for ci in range(NTC):
                    b1 = ctx.pb[(gi % 2) * 2]
                    b3 = ctx.pb[(gi % 2) * 2 + 1]
                    P.phase()
                    if ci == 0:
                        P.wait("tensor", tk13[ob])
                        if ob + 1 < NDT:
                            tk13[ob + 1] = load13(ob + 1)
                    lo = ci * TC
                    for k in range(NDT):
                        P.t(lambda t, b1=b1, k=k, ob=ob, lo=lo, xin=xin:
                            t.matmul(b1[:], wsl13[ob % 2][:, k, 0:128],
                                     xin[k][:, lo:lo + TC], start=(k == 0),
                                     stop=(k == NDT - 1)))
                    for k in range(NDT):
                        P.t(lambda t, b3=b3, k=k, ob=ob, lo=lo, xin=xin:
                            t.matmul(b3[:], wsl13[ob % 2][:, k, 128:256],
                                     xin[k][:, lo:lo + TC], start=(k == 0),
                                     stop=(k == NDT - 1)))
                    if pend is not None:
                        (pob, plo), pb1, pb3 = pend
                        P.s(lambda s, pb1=pb1: s.activation(tmpe[:], pb1[:],
                                                            AF.Silu))
                        P.phase()
                        P.v(lambda v, pob=pob, plo=plo, pb3=pb3:
                            v.tensor_tensor(bufB[pob][:, plo:plo + TC], tmpe[:],
                                            pb3[:], ALU.mult))
                    pend = ((ob, lo), b1, b3)
                    gi += 1
            P.phase()
            (pob, plo), pb1, pb3 = pend
            P.s(lambda s: s.activation(tmpe[:], pb1[:], AF.Silu))
            P.phase()
            P.v(lambda v: v.tensor_tensor(bufB[pob][:, plo:plo + TC], tmpe[:],
                                          pb3[:], ALU.mult))
            P.phase()

            def load2(ob, e=e):
                return P.dma(wsl[ob % 2][:], wdr2[e, ob], ch="w", queue="gpsimd")
            proj_sliced(P, ctx, lambda ob: wsl[ob % 2], load2, bufB,
                        evict_add_into(P, ctx.X))


def host_moe_w_sliced(w):
    a = host_wT2(w)
    return np.ascontiguousarray(
        a.reshape(128, NDT, NDT, 128).transpose(2, 0, 1, 3))


def build_L2_ffn(P, ctx, tag="L2f"):
    build_moe(P, ctx, tag, "l2")


def host_L2_ffn(inputs, tag="L2f"):
    w1 = np.asarray(inputs["L2_ffn_w1"])
    w2 = np.asarray(inputs["L2_ffn_w2"])
    w3 = np.asarray(inputs["L2_ffn_w3"])
    brows = np.stack([np.concatenate([np.asarray(inputs["L2_ffn_b1"])[e],
                                      np.asarray(inputs["L2_ffn_b2"])[e],
                                      np.asarray(inputs["L2_ffn_b3"])[e]])
                      for e in range(8)])[:, None, :]
    return {
        tag + "_gw": host_wT2(np.asarray(inputs["L2_ffn_gate_w"]).T),
        tag + "_gb": np.asarray(inputs["L2_ffn_gate_b"])[None, :],
        tag + "_brows": brows,
        tag + "_w0": np.stack([host_moe_w_sliced(w1[e]) for e in range(8)]),
        tag + "_w1": np.stack([host_moe_w_sliced(w2[e]) for e in range(8)]),
        tag + "_w2": np.stack([host_moe_w_sliced(w3[e]) for e in range(8)]),
    }


def build_L3_ffn(P, ctx, tag="L3f"):
    build_moe(P, ctx, tag, "l3")


def host_L3_ffn(inputs, tag="L3f"):
    w1 = np.asarray(inputs["L3_ffn_w1"])
    w2 = np.asarray(inputs["L3_ffn_w2"])
    w3 = np.asarray(inputs["L3_ffn_w3"])
    w13 = []
    for e in range(8):
        a1 = host_moe_w_sliced(w1[e])
        a3 = host_moe_w_sliced(w3[e])
        w13.append(np.concatenate([a1, a3], axis=3))
    a1 = host_moe_w_sliced(np.asarray(inputs["L3_ffn_w1s"])[0])
    a3 = host_moe_w_sliced(np.asarray(inputs["L3_ffn_w3s"])[0])
    w13.append(np.concatenate([a1, a3], axis=3))
    w2l = [host_moe_w_sliced(w2[e]) for e in range(8)]
    w2l.append(host_moe_w_sliced(np.asarray(inputs["L3_ffn_w2s"])[0]))
    return {
        tag + "_gw": host_wT2(np.asarray(inputs["L3_ffn_gate_w"]).T),
        tag + "_w13": np.stack(w13),
        tag + "_w2": np.stack(w2l),
    }


INNER = 2048
NIT = INNER // 128
NST = 16
DTR = 64


def build_mamba(P, ctx, tag, l3=False):
    win_dr = P.dram(tag + "_win", [2 * NIT, 128, NDT, 128], F32,
                    kind="ExternalInput").ap()
    wout_dr = P.dram(tag + "_wout", [NDT, 128, NIT, 128], F32,
                     kind="ExternalInput").ap()
    wx_dr = P.dram(tag + "_wx", [128, NIT, 96], F32, kind="ExternalInput").ap()
    wdt_dr = P.dram(tag + "_wdt", [64, NIT, 128], F32, kind="ExternalInput").ap()
    cvec_dr = P.dram(tag + "_cvec", [128, NIT, 23], F32, kind="ExternalInput").ap()
    xs_d = P.dram(tag + "_xs", [INNER, T], BF16).ap()
    gate_d = P.dram(tag + "_gate", [INNER, T], BF16).ap()
    u_d = P.dram(tag + "_u", [INNER, T], BF16).ap()
    delta_d = P.dram(tag + "_delta", [INNER, T], BF16).ap()
    y_d = P.dram(tag + "_y", [INNER, T], BF16).ap()

    wsl = [ctx.pool("moe_wsl%d" % i, [128, NDT, 128], BF16) for i in range(2)]
    wxsb = P.sb(tag + "_wx_sb", [128, NIT, 96], BF16)
    wdtsb = P.sb(tag + "_wdt_sb", [64, NIT, 128], BF16)
    cvec = P.sb(tag + "_cvec_sb", [128, NIT, 23])
    tk_wx = P.dma(wxsb[:], wx_dr, ch="w", queue="gpsimd")
    tk_wdt = P.dma(wdtsb[:], wdt_dr, ch="w", queue="gpsimd")
    tk_cv = P.dma(cvec[:], cvec_dr, ch="wsmall")
    band = ctx.pool("ln_rows", [128, 4096])
    xs_pad = band[:, 0:2051]
    u_bf = band[:, 2052:3076].bitcast(BF16)
    projt = ctx.pool("mm_projt", [96, T])
    dtlr_b = ctx.pool("mm_dtlr", [64, T], BF16)
    stage = ctx.pool("mm_stage", [128, 2, TC], BF16)
    ws = ctx.carve(8192)
    d_t = ws[:, 0:2048]
    du_t = ws[:, 2048:4096]
    y_t = ws[:, 4096:6144]
    h_t = ws[:, 6144:8192]
    er = ctx.pool("mm_er", [128, 128])
    acol = ctx.pool("mm_acol", [128, NIT, NST])

    def ev_stage(dst_d):
        def ev(ob, lo, hi, psum, eng):
            sl = stage[:, (lo // TC) % 2, 0:hi - lo]
            P.v(lambda v: v.tensor_copy(sl, psum[:, 0:hi - lo]))
            P.phase()
            P.dma(dst_d[ob * 128:(ob + 1) * 128, lo:hi], sl, ch="mmst")
        return ev

    tk = {0: P.dma(wsl[0][:], win_dr[0], ch="w", queue="gpsimd")}
    stage_tk = {}
    for ob in range(2 * NIT):
        dst = xs_d if ob < NIT else gate_d
        o2 = ob % NIT
        for ci in range(NTC):
            P.phase()
            if ci == 0:
                P.wait("tensor", tk[ob])
                if ob + 1 < 2 * NIT:
                    tk[ob + 1] = P.dma(wsl[(ob + 1) % 2][:], win_dr[ob + 1],
                                       ch="w", queue="gpsimd")
            pbank = ctx.pb[ci % 4]
            lo = ci * TC
            for k in range(NDT):
                P.t(lambda t, pb=pbank, k=k, ob=ob, lo=lo:
                    t.matmul(pb[:], wsl[ob % 2][:, k, :], ctx.XN[k][:, lo:lo + TC],
                             start=(k == 0), stop=(k == NDT - 1)))
            P.phase()
            sl = stage[:, ci % 2, :]
            if (ob * NTC + ci) >= 2:
                P.wait("vector", stage_tk[(ob * NTC + ci) % 2])
            P.v(lambda v, sl=sl, pb=pbank: v.tensor_copy(sl, pb[:]))
            P.phase()
            stage_tk[(ob * NTC + ci) % 2] = P.dma(
                dst[o2 * 128:(o2 + 1) * 128, lo:lo + TC], sl, ch="mmst")
    P.phase()
    P.wait_all("vector", "mmst")
    P.wait_all("gpsimd", "mmst")
    for e in ("vector", "scalar", "tensor"):
        P.wait_all(e, "wsmall")
    P.phase()

    for c in range(NIT):
        tkx = P.dma(xs_pad[:, 3:2051], xs_d[c * 128:(c + 1) * 128, :], ch="mmld",
                    queue="gpsimd")
        if c == 0:
            P.phase()
            P.wait("tensor", tk_wx)
        P.phase()
        P.v(lambda v: v.memset(xs_pad[:, 0:3], 0.0))
        P.wait("vector", tkx)
        P.phase()
        P.v(lambda v, c=c: v.tensor_scalar(d_t[:], xs_pad[:, 0:2048],
                                           cvec[:, c, 0:1], None, ALU.mult))
        P.phase()
        for j in range(1, 4):
            P.v(lambda v, c=c, j=j: v.scalar_tensor_tensor(
                d_t[:], xs_pad[:, j:2048 + j], cvec[:, c, j:j + 1], d_t[:],
                ALU.mult, ALU.add))
            P.phase()
        P.s(lambda s, c=c: s.activation(u_bf[:, 0:2048], d_t[:], AF.Silu,
                                        bias=cvec[:, c, 4:5]))
        P.phase()
        tku = P.dma(u_d[c * 128:(c + 1) * 128, :], u_bf[:, 0:2048], ch="mmst")
        for ci in range(NTC):
            P.t(lambda t, c=c, ci=ci:
                t.matmul(ctx.pb[4 + ci][0:96, :], wxsb[:, c, :],
                         u_bf[:, ci * TC:(ci + 1) * TC],
                         start=(c == 0), stop=(c == NIT - 1),
                         skip_group_check=True))
        P.phase()
        P.wait("gpsimd", tku)
        P.phase()
    for ci in range(NTC):
        P.s(lambda s, ci=ci: s.copy(projt[:, ci * TC:(ci + 1) * TC],
                                    ctx.pb[4 + ci][0:96, :]))
    P.phase()
    P.v(lambda v: v.tensor_copy(dtlr_b[:], projt[0:64, :]))
    P.phase()

    for dt in range(NIT):
        for ci in range(NTC):
            pbank = ctx.pb[ci % 2]
            P.phase()
            if dt == 0 and ci == 0:
                P.wait("tensor", tk_wdt)
            P.t(lambda t, dt=dt, ci=ci, pb=pbank:
                t.matmul(pb[:], wdtsb[:, dt, :], dtlr_b[:, ci * TC:(ci + 1) * TC],
                         start=True, stop=True))
            P.phase()
            lo = ci * TC
            P.v(lambda v, dt=dt, pb=pbank, lo=lo:
                v.tensor_scalar(d_t[:, lo:lo + TC], pb[:], cvec[:, dt, 5:6],
                                None, ALU.add))
            P.phase()
            P.s(lambda s, lo=lo: s.activation(du_t[:, lo:lo + TC],
                                              d_t[:, lo:lo + TC], AF.Exp))
            P.phase()
            P.s(lambda s, lo=lo: s.activation(d_t[:, lo:lo + TC],
                                              du_t[:, lo:lo + TC], AF.Ln,
                                              bias=1.0))
            P.phase()
        P.v(lambda v: v.tensor_copy(stage[:, 0, :].bitcast(F32).bitcast(BF16)
                                    if False else u_bf[:, 0:2048], d_t[:]))
        P.phase()
        tkd = P.dma(delta_d[dt * 128:(dt + 1) * 128, :], u_bf[:, 0:2048], ch="mmst")
        P.phase()
        P.wait("gpsimd", tkd)
        P.phase()

    P.phase()
    P.s(lambda s: s.activation(acol[:], cvec[:, :, 7:23], AF.Exp))
    P.phase()
    P.v(lambda v: v.tensor_scalar_mul(acol[:].rearrange("p a b -> p (a b)"),
                                      acol[:].rearrange("p a b -> p (a b)"), -1.0))
    P.phase()

    for dt in range(NIT):
        tkD = P.dma(d_t[:], delta_d[dt * 128:(dt + 1) * 128, :], ch="mmld",
                    queue="gpsimd")
        tkU = P.dma(du_t[:], u_d[dt * 128:(dt + 1) * 128, :], ch="mmld",
                    queue="gpsimd")
        P.phase()
        P.wait("vector", tkD)
        P.wait("vector", tkU)
        P.phase()
        P.v(lambda v: v.tensor_tensor(du_t[:], d_t[:], du_t[:], ALU.mult))
        P.phase()
        for n in range(NST):
            P.v(lambda v, n=n: v.tensor_copy(
                er[:], ctx.identf[:, 64 + n:65 + n].to_broadcast([128, 128])))
            P.phase()
            for ci in range(NTC):
                P.t(lambda t, ci=ci: t.matmul(ctx.pb[ci][:],
                                              er[0:96, :],
                                              projt[:, ci * TC:(ci + 1) * TC],
                                              start=True, stop=True))
            P.v(lambda v, dt=dt, n=n: v.tensor_scalar(h_t[:], d_t[:],
                                                      acol[:, dt, n:n + 1], None,
                                                      ALU.mult))
            P.phase()
            P.s(lambda s: s.activation(h_t[:], h_t[:], AF.Exp))
            dbu_b = band[:, 0:1024].bitcast(BF16)
            for ci in range(NTC):
                P.v(lambda v, ci=ci, dbu_b=dbu_b: v.tensor_tensor(
                    dbu_b[:, ci * TC:(ci + 1) * TC],
                    du_t[:, ci * TC:(ci + 1) * TC], ctx.pb[ci][:], ALU.mult))
            P.phase()
            P.v(lambda v, dbu_b=dbu_b: v.tensor_tensor_scan(
                h_t[:], h_t[:], dbu_b[:], 0.0, ALU.mult, ALU.add))
            P.phase()
            P.v(lambda v, n=n: v.tensor_copy(
                er[:], ctx.identf[:, 80 + n:81 + n].to_broadcast([128, 128])))
            P.phase()
            for ci in range(NTC):
                P.t(lambda t, ci=ci: t.matmul(ctx.pb[ci][:], er[0:96, :],
                                              projt[:, ci * TC:(ci + 1) * TC],
                                              start=True, stop=True))
            P.phase()
            if n == 0:
                for ci in range(NTC):
                    P.v(lambda v, ci=ci: v.tensor_tensor(
                        y_t[:, ci * TC:(ci + 1) * TC],
                        h_t[:, ci * TC:(ci + 1) * TC], ctx.pb[ci][:], ALU.mult))
            else:
                for ci in range(NTC):
                    P.v(lambda v, ci=ci: v.tensor_tensor(
                        du_t[:, ci * TC:(ci + 1) * TC] if False else
                        h_t[:, ci * TC:(ci + 1) * TC],
                        h_t[:, ci * TC:(ci + 1) * TC], ctx.pb[ci][:], ALU.mult))
                P.phase()
                P.v(lambda v: v.tensor_tensor(y_t[:], y_t[:], h_t[:], ALU.add))
            P.phase()
            if n == NST - 1:
                pass
        tkU2 = P.dma(u_bf[:, 0:2048], u_d[dt * 128:(dt + 1) * 128, :], ch="mmld",
                     queue="gpsimd")
        P.phase()
        P.wait("vector", tkU2)
        P.phase()
        P.v(lambda v, dt=dt: v.scalar_tensor_tensor(
            h_t[:], u_bf[:, 0:2048], cvec[:, dt, 6:7], y_t[:], ALU.mult, ALU.add))
        P.phase()
        tkG = P.dma(u_bf[:, 0:2048], gate_d[dt * 128:(dt + 1) * 128, :], ch="mmld",
                    queue="gpsimd")
        P.phase()
        P.wait("scalar", tkG)
        P.phase()
        P.s(lambda s: s.activation(d_t[:], u_bf[:, 0:2048], AF.Silu))
        P.phase()
        P.v(lambda v: v.tensor_tensor(h_t[:], h_t[:], d_t[:], ALU.mult))
        P.phase()
        P.v(lambda v: v.tensor_copy(u_bf[:, 0:2048], h_t[:]))
        P.phase()
        tkY = P.dma(y_d[dt * 128:(dt + 1) * 128, :], u_bf[:, 0:2048], ch="mmst")
        P.phase()
        P.wait("gpsimd", tkY)
        P.phase()

    wout_b = [ctx.pool("mm_wout%d" % i, [128, NIT, 128], BF16) for i in range(2)]
    yts = ctx.pool("mm_ybuf", [128, NIT, TC], BF16)
    tko = {}
    tko[(0, 0)] = P.dma(wout_b[0][:], wout_dr[0], ch="w", queue="gpsimd")
    for ci in range(NTC):
        lo = ci * TC
        tky = []
        for k in range(NIT):
            tky.append(P.dma(yts[:, k, :], y_d[k * 128:(k + 1) * 128, lo:lo + TC],
                             ch="mmld", queue="gpsimd"))
        P.phase()
        P.wait("tensor", tky[-1])
        P.phase()
        for ob in range(NDT):
            nxt = (ci, ob + 1) if ob + 1 < NDT else (ci + 1, 0)
            P.phase()
            P.wait("tensor", tko[(ci, ob)])
            if nxt[0] < NTC:
                tko[nxt] = P.dma(wout_b[(nxt[0] * NDT + nxt[1]) % 2][:],
                                 wout_dr[nxt[1]], ch="w", queue="gpsimd")
            pbank = ctx.pb[ob % 4]
            for k in range(NIT):
                P.t(lambda t, pb=pbank, k=k, ci=ci, ob=ob:
                    t.matmul(pb[:], wout_b[(ci * NDT + ob) % 2][:, k, :],
                             yts[:, k, :], start=(k == 0), stop=(k == NIT - 1)))
            P.phase()
            P.v(lambda v, ob=ob, lo=lo, pb=pbank:
                v.tensor_tensor(ctx.X[ob][:, lo:lo + TC], ctx.X[ob][:, lo:lo + TC],
                                pb[:], ALU.add))
    P.phase()


def host_mamba(inputs, tag, pfx, l3=False):
    W_in = np.asarray(inputs[pfx + "_W_in"])
    W_out = np.asarray(inputs[pfx + "_W_out"])
    W_x = np.asarray(inputs[pfx + ("_W_xproj" if l3 else "_W_x")])
    W_dt = np.asarray(inputs[pfx + "_W_dt"])
    conv_w = np.asarray(inputs[pfx + ("_W_conv" if l3 else "_conv_w")])
    conv_b = np.asarray(inputs[pfx + ("_b_conv" if l3 else "_conv_b")])
    dt_b = np.asarray(inputs[pfx + ("_b_dt" if l3 else "_dt_bias")])
    A_log = np.asarray(inputs[pfx + "_A_log"])
    D_skip = np.asarray(inputs[pfx + "_D_skip"])
    a = host_wT(W_in)
    win = np.ascontiguousarray(a.reshape(128, NDT, 2 * NIT, 128)
                               .transpose(2, 0, 1, 3))
    b = host_wT(W_out)
    wout = np.ascontiguousarray(b.reshape(128, NIT, NDT, 128).transpose(2, 0, 1, 3))
    wx = np.ascontiguousarray(W_x.T.reshape(NIT, 128, 96).transpose(1, 0, 2))
    wdt = np.ascontiguousarray(W_dt.T.reshape(64, NIT, 128))
    cvec = np.zeros((128, NIT, 23), np.float32)
    cw = conv_w.reshape(NIT, 128, 4).transpose(1, 0, 2)
    cvec[:, :, 0:4] = cw
    cvec[:, :, 4] = conv_b.reshape(NIT, 128).T
    cvec[:, :, 5] = dt_b.reshape(NIT, 128).T
    cvec[:, :, 6] = D_skip.reshape(NIT, 128).T
    cvec[:, :, 7:23] = A_log.reshape(NIT, 128, NST).transpose(1, 0, 2)
    return {tag + "_win": win, tag + "_wout": wout, tag + "_wx": wx,
            tag + "_wdt": wdt, tag + "_cvec": cvec}


def build_L0_attn_a(P, ctx):
    build_mamba(P, ctx, "L0aa", l3=False)


def host_L0_attn_a(inputs):
    return host_mamba(inputs, "L0aa", "L0_attn_a", l3=False)


def build_L3_attn_a(P, ctx):
    tag = "L3aa"
    band = ctx.pool("ln_rows", [128, 4096])
    b_r = band[0:1, 0:T]
    rstd_r = band[0:1, T:2 * T]
    lw_dr = P.dram(tag + "_lnwb", [128, 2 * NDT], F32, kind="ExternalInput").ap()
    lw = P.sb(tag + "_lnwb_sb", [128, 2 * NDT])
    P.dma(lw[:], lw_dr, ch="wsmall")
    P.phase()
    for k in range(NDT):
        P.v(lambda v, k=k: v.scalar_tensor_tensor(
            ctx.X[k][:], ctx.XN[k][:], 1.0, ctx.X[k][:], ALU.mult, ALU.add))
    P.phase()
    P.wait_all("vector", "wsmall")
    r2 = ctx.arena[0:1, 4096:6144]
    P.phase()
    P.v(lambda v: v.tensor_tensor(r2, rstd_r, rstd_r, ALU.mult))
    P.phase()
    P.v(lambda v: v.tensor_scalar(r2, r2, -1e-5, 1.0 + 1e-5, ALU.mult, ALU.add))
    P.phase()
    P.s(lambda s: s.activation(r2, r2, AF.Sqrt))
    P.phase()
    P.v(lambda v: v.reciprocal(r2, r2))
    P.phase()
    for ci in range(NTC):
        lo = ci * TC
        P.t(lambda t, lo=lo, ci=ci: t.matmul(ctx.pb[4 + ci % 2][:],
                                             ctx.ones_row_f[:],
                                             r2[:, lo:lo + TC], start=True,
                                             stop=True))
        P.phase()
        for k in range(NDT):
            P.v(lambda v, k=k, lo=lo, ci=ci: v.tensor_tensor(
                ctx.arena[:, (k % 4) * TC:(k % 4) * TC + TC],
                ctx.XN[k][:, lo:lo + TC], ctx.pb[4 + ci % 2][:], ALU.mult))
            if k % 4 == 3 or k == NDT - 1:
                P.phase()
                for k2 in range(k - k % 4, k + 1):
                    P.v(lambda v, k2=k2, lo=lo: v.tensor_scalar(
                        ctx.XN[k2][:, lo:lo + TC],
                        ctx.arena[:, (k2 % 4) * TC:(k2 % 4) * TC + TC],
                        lw[:, k2:k2 + 1], lw[:, NDT + k2:NDT + k2 + 1],
                        ALU.mult, ALU.add))
                P.phase()
    P.phase()
    build_mamba(P, ctx, tag, l3=True)


def host_L3_attn_a(inputs):
    d = host_mamba(inputs, "L3aa", "L3_attn_a", l3=True)
    lnwb = np.concatenate([host_col(np.asarray(inputs["L3_attn_a_ln_w"])),
                           host_col(np.asarray(inputs["L3_attn_a_ln_b"]))], axis=1)
    d["L3aa_lnwb"] = lnwb
    return d



def build_L2_attn_a(P, ctx):
    tag = "L2Aa"
    H, hd, M = 16, 64, 256
    dn = float(hd) ** (-0.25)
    ratio = float(M) ** (-0.5)
    EPS = 1e-4
    wb = ctx.pool("attn_wbuf", [128, NDT, D], BF16)
    dr_v = P.dram(tag + "_wv", [128, NDT, D], F32, kind="ExternalInput").ap()
    dr_o = P.dram(tag + "_wo", [128, NDT, D], F32, kind="ExternalInput").ap()
    dr_qh = P.dram(tag + "_wqh", [H, 128, NDT, 64], F32, kind="ExternalInput").ap()
    dr_kh = P.dram(tag + "_wkh", [H, 128, NDT, 64], F32, kind="ExternalInput").ap()
    wqh = [ctx.pool("attn_wqh%d" % i, [128, NDT, 64], BF16) for i in range(2)]
    wkh = [ctx.pool("attn_wkh%d" % i, [128, NDT, 64], BF16) for i in range(2)]
    pt_dr = P.dram(tag + "_pt", [64, M], F32, kind="ExternalInput").ap()
    ptb = P.sb(tag + "_pt_sb", [64, M], BF16)
    bo_dr = P.dram(tag + "_bo", [128, NDT], F32, kind="ExternalInput").ap()
    bo_sb = P.sb(tag + "_bo_sb", [128, NDT])
    tk_v = P.dma(wb[:], dr_v, ch="w", queue="gpsimd")
    qk_tk = {0: (P.dma(wqh[0][:], dr_qh[0], ch="w", queue="gpsimd"),
                 P.dma(wkh[0][:], dr_kh[0], ch="w", queue="gpsimd"))}
    P.dma(ptb[:], pt_dr, ch="wsmall", queue="gpsimd")
    P.dma(bo_sb[:], bo_dr, ch="wsmall")
    arena_b = ctx.arena[:].bitcast(BF16)
    Vtok = [arena_b[:, ts * D:(ts + 1) * D] for ts in range(NTT)]
    yth = ctx.pool("attn_yth", [128, 2, T], BF16)
    YT = [yth[:, k, :] for k in range(2)]
    qkband = ctx.pool("attn_qkh", [64, 2 * T], BF16)
    QH = qkband[:, 0:T]
    KH = qkband[:, T:2 * T]
    band = ctx.pool("ln_rows", [128, 4096])
    e1_r = band[0:1, 0:T]
    den_r = band[0:1, T:2 * T]
    gmxcol = ctx.pool("l2a_gmx", [128, 4])
    ktcols = ctx.pool("l2a_ktc", [128, 2], BF16)
    qpT = ctx.pool("l2a_qpT", [128, 2, T], BF16)
    kptok = ctx.pool("l2a_kp", [128, NTT, M], BF16)
    cols = ctx.pool("l2a_cols", [128, 40])
    ctxsb = ctx.pool("l2a_ctx", [128, 2, 64], BF16)
    tmpc = ctx.pool("attn_tmp", [128, TC])
    cc_in = P.dram(tag + "_ccin", [1, 8], F32).ap()
    cc_out = P.dram(tag + "_ccout", [1, 8], F32).ap()
    ccsem = P.sem("l2a_cc")

    P.phase()
    for e in ("vector", "scalar", "tensor"):
        P.wait_all(e, "wsmall")
    P.phase()

    def evv(ts, lo, hi, psum, eng):
        P.v(lambda v: v.tensor_copy(Vtok[ts][:, lo:hi], psum[:, 0:hi - lo]))
    proj_tok(P, ctx, wb, D, D, ctx.XN, evv, waits=[tk_v])
    P.phase()
    tk_o = P.dma(wb[:], dr_o, ch="w", queue="gpsimd")
    P.phase()

    def get_qk(h):
        if h + 1 < H:
            qk_tk[h + 1] = (P.dma(wqh[(h + 1) % 2][:], dr_qh[h + 1], ch="w",
                                  queue="gpsimd"),
                            P.dma(wkh[(h + 1) % 2][:], dr_kh[h + 1], ch="w",
                                  queue="gpsimd"))
        P.phase()
        P.wait("tensor", qk_tk[h][0])
        P.wait("tensor", qk_tk[h][1])
        P.phase()
        proj_head(P, ctx, wqh[h % 2], 0, ctx.XN, QH, banks=(0, 1))
        proj_head(P, ctx, wkh[h % 2], 0, ctx.XN, KH, banks=(0, 1))

    def diag_mx(src, want_mx):
        for ts in range(NTT):
            P.phase()
            P.t(lambda t, ts=ts: t.matmul(ctx.pb[ts % 2][:, 0:M],
                                          src[:, ts * 128:(ts + 1) * 128],
                                          ptb[:], start=True, stop=True))
            P.s(lambda s, ts=ts: s.activation(tmpc[0:64, :],
                                              src[:, ts * 128:(ts + 1) * 128]
                                              .bitcast(BF16)[:, 0:128]
                                              if False else tmpc[0:64, 0:128],
                                              AF.Square)) if False else None
            P.phase()
            if want_mx:
                P.v(lambda v, ts=ts: v.reduce_max(cols[:, ts:ts + 1],
                                                  ctx.pb[ts % 2][:, 0:M], axis=AX.X))
            else:
                P.v(lambda v, ts=ts: v.reduce_max(cols[:, 32 + ts % 8:33 + ts % 8]
                                                  if False else cols[:, ts:ts + 1],
                                                  ctx.pb[ts % 2][:, 0:M], axis=AX.X))
            P.phase()

    def sq_cols(src, dst_lo):
        for ts in range(NTT):
            P.s(lambda s, ts=ts: s.activation(
                tmpc[0:64, 0:128], src[:, ts * 128:(ts + 1) * 128], AF.Square))
            P.phase()
            P.t(lambda t, ts=ts: t.matmul(ctx.pb[2][0:1, 0:128],
                                          ctx.ones_col[0:64, :], tmpc[0:64, 0:128],
                                          start=True, stop=True))
            P.phase()
            P.v(lambda v, ts=ts: v.tensor_copy(tmpc[0:1, 128:256],
                                               ctx.pb[2][0:1, 0:128]))
            P.phase()
            P.t(lambda t, ts=ts: t.matmul(ctx.pb[3][:, ts:ts + 1],
                                          tmpc[0:1, 128:256],
                                          ctx.ones_row_f[0:1, 0:1],
                                          start=True, stop=True,
                                          skip_group_check=True))
        P.phase()
        P.v(lambda v: v.tensor_scalar(cols[:, dst_lo:dst_lo + 16],
                                      ctx.pb[3][:, 0:16], 0.5 * dn * dn, None,
                                      ALU.mult))
        P.phase()

    loc_mx = tmpc[0:1, 200:201]
    for h in range(H):
        get_qk(h)
        diag_mx(KH, True)
        P.v(lambda v: v.reduce_max(cols[:, 32:33], cols[:, 0:16], axis=AX.X))
        P.phase()
        P.t(lambda t: t.matmul(ctx.pb[2][0:1, 0:128], cols[:, 32:33],
                               ctx.identf[:], start=True, stop=True))
        P.phase()
        P.v(lambda v: v.tensor_copy(tmpc[0:1, 0:128], ctx.pb[2][0:1, 0:128]))
        P.phase()
        if h == 0:
            P.v(lambda v: v.reduce_max(loc_mx, tmpc[0:1, 0:128], axis=AX.X))
        else:
            P.v(lambda v: v.reduce_max(tmpc[0:1, 128:129], tmpc[0:1, 0:128],
                                       axis=AX.X))
            P.phase()
            P.v(lambda v: v.tensor_tensor(loc_mx, loc_mx, tmpc[0:1, 128:129],
                                          ALU.max))
        P.phase()
    P.v(lambda v: v.tensor_copy(tmpc[0:1, 0:8], loc_mx.to_broadcast([1, 8])))
    P.phase()
    tkc = P.dma(cc_in, tmpc[0:1, 0:8], ch="ccst", queue="gpsimd")
    P.phase()
    def _cc(g, tkc=tkc):
        g.wait_ge(tkc[0], tkc[1])
        g.collective_compute("AllReduce", ALU.max, replica_groups=[list(range(8))],
                             ins=[cc_in.opt()], outs=[cc_out.opt()]).then_inc(ccsem)
        g.wait_ge(ccsem, 1)
        return None
    P.g(_cc)
    P.dma(tmpc[0:1, 0:8], cc_out, ch="ccld", queue="gpsimd")
    P.phase()
    P.wait_all("vector", "ccld")
    P.wait_all("tensor", "ccld")
    P.phase()
    P.t(lambda t: t.matmul(ctx.pb[2][:, 0:1], ctx.ones_row_f[:], tmpc[0:1, 0:1],
                           start=True, stop=True))
    P.phase()
    P.v(lambda v: v.tensor_copy(gmxcol[:, 0:1], ctx.pb[2][:, 0:1]))
    P.phase()

    def finish(h):
        sq_cols(KH, 16)
        for ts in range(NTT):
            P.phase()
            P.t(lambda t, ts=ts: t.matmul(ctx.pb[ts % 2][:, 0:M],
                                          KH[:, ts * 128:(ts + 1) * 128],
                                          ptb[:], start=True, stop=True))
            P.phase()
            P.v(lambda v, ts=ts: v.tensor_scalar(cols[:, 32:33],
                                                 cols[:, 16 + ts:17 + ts],
                                                 gmx[0:1, 0:1].to_broadcast([128, 1])
                                                 if False else 0.0, None, ALU.add)
                ) if False else None
            P.v(lambda v, ts=ts: v.tensor_tensor(cols[:, 33:34],
                                                 cols[:, 16 + ts:17 + ts],
                                                 gmxcol[:, 0:1], ALU.add))
            P.phase()
            P.v(lambda v: v.tensor_scalar_mul(cols[:, 33:34], cols[:, 33:34], -1.0))
            P.phase()
            P.s(lambda s, ts=ts: s.activation(tmpc[:, 0:M], ctx.pb[ts % 2][:, 0:M],
                                              AF.Exp, bias=cols[:, 33:34],
                                              scale=dn if False else 1.0))
            P.phase()
            P.v(lambda v, ts=ts: v.tensor_scalar(kptok[:, ts, :], tmpc[:, 0:M],
                                                 EPS, ratio, ALU.add, ALU.mult))
            P.phase()
        P.phase()
        for ts in range(NTT):
            P.t(lambda t, ts=ts: t.matmul(ctx.pb[4][0:1, 0:M], ctx.ones_col_b[:],
                                          kptok[:, ts, :], start=(ts == 0),
                                          stop=(ts == NTT - 1),
                                          skip_group_check=True))
            for mh in range(2):
                P.t(lambda t, ts=ts, mh=mh, h=h: t.matmul(
                    ctx.pb[5 + mh][0:128, 0:64],
                    kptok[:, ts, mh * 128:(mh + 1) * 128],
                    Vtok[ts][:, h * 64:(h + 1) * 64], start=(ts == 0),
                    stop=(ts == NTT - 1), skip_group_check=True))
        P.phase()
        kcum_r = tmpc[0:1, 256:512]
        P.v(lambda v: v.tensor_copy(kcum_r[:, 0:M], ctx.pb[4][0:1, 0:M]))
        for mh in range(2):
            P.v(lambda v, mh=mh: v.tensor_copy(ctxsb[:, mh, 0:64],
                                               ctx.pb[5 + mh][0:128, 0:64]))
        P.phase()
        for mh in range(2):
            P.t(lambda t, mh=mh: t.matmul(ctx.pb[2][:, mh:mh + 1],
                                          kcum_r[:, mh * 128:(mh + 1) * 128],
                                          ctx.ones_row_f[0:1, 0:1], start=True,
                                          stop=True, skip_group_check=True))
        P.phase()
        P.v(lambda v: v.tensor_copy(ktcols[:, 0:2], ctx.pb[2][:, 0:2]))
        P.phase()
        sq_cols(QH, 16)
        diag_mx(QH, True)
        P.v(lambda v: v.tensor_tensor(cols[:, 32:33], cols[:, 0:1], cols[:, 0:1],
                                      ALU.max)) if False else None
        for ts in range(NTT):
            P.v(lambda v, ts=ts: v.tensor_tensor(cols[:, 32:33],
                                                 cols[:, ts:ts + 1],
                                                 cols[:, 16 + ts:17 + ts], ALU.add))
            P.phase()
            P.s(lambda s: s.activation(cols[:, 33:34], cols[:, 32:33], AF.Exp,
                                       scale=-1.0))
            P.phase()
            P.t(lambda t, ts=ts: t.matmul(ctx.pb[2][0:1, 0:128], cols[:, 33:34],
                                          ctx.identf[:], start=True, stop=True))
            P.phase()
            P.v(lambda v, ts=ts: v.tensor_copy(e1_r[:, ts * 128:(ts + 1) * 128],
                                               ctx.pb[2][0:1, 0:128]))
            P.phase()
        for mh in range(2):
            for ci in range(NTC):
                P.phase()
                P.t(lambda t, mh=mh, ci=ci: t.matmul(
                    ctx.pb[ci % 2][:], ptb[:, mh * 128:(mh + 1) * 128],
                    QH[:, ci * TC:(ci + 1) * TC], start=True, stop=True))
                P.t(lambda t, ci=ci: t.matmul(ctx.pb[2 + ci % 2][:],
                                              ctx.ones_row_f[:],
                                              e1_r[:, ci * TC:(ci + 1) * TC],
                                              start=True, stop=True))
                P.phase()
                P.s(lambda s, ci=ci: s.activation(tmpc[:], ctx.pb[ci % 2][:],
                                                  AF.Exp))
                P.phase()
                P.v(lambda v, ci=ci: v.tensor_tensor(tmpc[:], tmpc[:],
                                                     ctx.pb[2 + ci % 2][:],
                                                     ALU.mult))
                P.phase()
                P.v(lambda v, mh=mh, ci=ci: v.tensor_scalar(
                    qpT[:, mh, ci * TC:(ci + 1) * TC], tmpc[:], EPS * 0.0 + EPS
                    if False else EPS, ratio, ALU.add, ALU.mult))
                P.phase()
        P.phase()
        for mh in range(2):
            for ci in range(NTC):
                P.t(lambda t, mh=mh, ci=ci: t.matmul(
                    ctx.pb[4][0:1, ci * 128:ci * 128 + 128] if False else
                    ctx.pb[4 + ci % 2][0:1, :],
                    ktcols[:, mh:mh + 1],
                    qpT[:, mh, ci * TC:(ci + 1) * TC], start=(mh == 0),
                    stop=(mh == 1), skip_group_check=True))
        P.phase()
        for ci in range(NTC):
            P.v(lambda v, ci=ci: v.tensor_copy(den_r[:, ci * TC:(ci + 1) * TC],
                                               ctx.pb[4 + ci % 2][0:1, :]))
        P.phase()
        P.v(lambda v: v.reciprocal(den_r[:], den_r[:]))
        P.phase()
        td, pbase = (h % 4) // 2, 64 * (h % 2)
        for ci in range(NTC):
            P.phase()
            for mh in range(2):
                P.t(lambda t, mh=mh, ci=ci: t.matmul(
                    ctx.pb[ci % 2][0:64, :], ctxsb[:, mh, 0:64],
                    qpT[:, mh, ci * TC:(ci + 1) * TC], start=(mh == 0),
                    stop=(mh == 1)))
            P.t(lambda t, ci=ci: t.matmul(ctx.pb[2 + ci % 2][0:64, :],
                                          ctx.ones_row_f[:, 0:64],
                                          den_r[:, ci * TC:(ci + 1) * TC],
                                          start=True, stop=True))
            P.phase()
            P.s(lambda s, ci=ci: s.copy(tmpc[0:64, :], ctx.pb[ci % 2][0:64, :]))
            P.phase()
            P.v(lambda v, ci=ci, td=td, pbase=pbase: v.tensor_tensor(
                YT[td][pbase:pbase + 64, ci * TC:(ci + 1) * TC], tmpc[0:64, :],
                ctx.pb[2 + ci % 2][0:64, :], ALU.mult))
            P.phase()

    def evo_half(half):
        def evo(ob, lo, hi, psum, eng):
            if half == 0:
                P.v(lambda v: v.scalar_tensor_tensor(ctx.X[ob][:, lo:hi],
                                                     psum[:, 0:hi - lo],
                                                     bo_sb[:, ob:ob + 1],
                                                     ctx.X[ob][:, lo:hi],
                                                     ALU.add, ALU.add))
            else:
                P.v(lambda v: v.tensor_tensor(ctx.X[ob][:, lo:hi],
                                              ctx.X[ob][:, lo:hi],
                                              psum[:, 0:hi - lo], ALU.add))
        return evo

    qk_tk[0] = (P.dma(wqh[0][:], dr_qh[0], ch="w", queue="gpsimd"),
                P.dma(wkh[0][:], dr_kh[0], ch="w", queue="gpsimd"))
    P.phase()
    for h in range(H):
        get_qk(h)
        finish(h)
        if h % 4 == 3:
            q = h // 4
            proj_fm(P, ctx, wb, D, D, YT, evo_half(q),
                    waits=[tk_o] if q == 0 else (), kt_list=[2 * q, 2 * q + 1])


def host_L2_attn_a(inputs):
    dn = 64.0 ** (-0.25)
    Pm = np.asarray(inputs["L2_attn_a_P"])
    return {
        "L2Aa_wqh": host_wT2_heads(inputs["L2_attn_a_Wq"]),
        "L2Aa_wkh": host_wT2_heads(inputs["L2_attn_a_Wk"]),
        "L2Aa_wv": host_wT2(inputs["L2_attn_a_Wv"]),
        "L2Aa_wo": host_wT2(inputs["L2_attn_a_Wo"]),
        "L2Aa_pt": np.ascontiguousarray(Pm.T * dn),
        "L2Aa_bo": host_col(inputs["L2_attn_a_bo"]),
    }



def build_L1_ffn(P, ctx, tag="L1f"):
    QA = 32767.0
    BLK = 32
    wraw = {"up": P.dram(tag + "_up", [24, 128, 1024], F32, kind="ExternalInput").ap(),
            "gate": P.dram(tag + "_gate", [24, 128, 1024], F32,
                           kind="ExternalInput").ap(),
            "down": P.dram(tag + "_down", [8, 128, 3072], F32,
                           kind="ExternalInput").ap()}
    wdq = {"up": P.dram(tag + "_dqup", [24, 8, 128, 128], BF16).ap(),
           "gate": P.dram(tag + "_dqgate", [24, 8, 128, 128], BF16).ap(),
           "down": P.dram(tag + "_dqdown", [8, 24, 128, 128], BF16).ap()}
    up_d = P.dram(tag + "_upact", [3072, T], BF16).ap()
    gate_d = P.dram(tag + "_gateact", [3072, T], BF16).ap()
    g2_d = P.dram(tag + "_g2act", [3072, T], BF16).ap()
    o_d = P.dram(tag + "_oact", [3072, T], BF16).ap()
    cc_in = P.dram(tag + "_ccin", [1, 8], F32).ap()
    cc_out = P.dram(tag + "_ccout", [1, 8], F32).ap()
    ccsem = P.sem(tag + "_cc")
    ccn = [0]
    band = ctx.pool("ln_rows", [128, 4096])
    srow = band[0:1, 0:256]
    cols = ctx.pool("l1f_cols", [128, 48])
    arena = ctx.arena
    stg = ctx.pool("mm_stage", [128, 2, TC], BF16)
    wsl = [ctx.pool("moe_wsl%d" % i, [128, NDT, 128], BF16) for i in range(2)]
    wsl24 = [ctx.pool("l1f_wsl%d" % i, [128, 24, 128], BF16) for i in range(2)]

    P.phase()
    P.v(lambda v: v.memset(srow[:], 0.0))
    P.phase()

    def allred_max():
        ccn[0] += 1
        tkc = P.dma(cc_in, srow[0:1, 0:8], ch="ccst", queue="gpsimd")
        P.phase()

        def _cc(g, tkc=tkc, n=ccn[0]):
            g.wait_ge(tkc[0], tkc[1])
            g.collective_compute("AllReduce", ALU.max,
                                 replica_groups=[list(range(8))],
                                 ins=[cc_in.opt()], outs=[cc_out.opt()]
                                 ).then_inc(ccsem)
            g.wait_ge(ccsem, n)
            return None
        P.g(_cc)
        P.dma(srow[0:1, 0:8], cc_out, ch="ccld", queue="gpsimd")
        P.phase()
        for e in ("vector", "scalar", "tensor"):
            P.wait_all(e, "ccld")
        P.phase()

    def col_to_slot(col, slot):
        P.phase()
        P.t(lambda t: t.matmul(ctx.pb[2][0:1, 0:128], col, ctx.identf[:],
                               start=True, stop=True))
        P.phase()
        P.v(lambda v: v.tensor_copy(srow[0:1, 16:144], ctx.pb[2][0:1, 0:128]))
        P.phase()
        P.v(lambda v, slot=slot: v.reduce_max(srow[0:1, slot:slot + 1],
                                              srow[0:1, 16:144], axis=AX.X))
        P.phase()

    def scales_from_slot(slot, rs_col, s_col):
        P.phase()
        P.v(lambda v, slot=slot: v.reciprocal(srow[0:1, 10:11],
                                              srow[0:1, slot:slot + 1]))
        P.phase()
        P.v(lambda v: v.tensor_scalar_mul(srow[0:1, 10:11], srow[0:1, 10:11], QA))
        P.phase()
        P.v(lambda v, slot=slot: v.tensor_scalar_mul(srow[0:1, 11:12],
                                                     srow[0:1, slot:slot + 1],
                                                     1.0 / QA))
        P.phase()
        P.t(lambda t: t.matmul(ctx.pb[2][:, 0:2], ctx.ones_row_f[:],
                               srow[0:1, 10:12], start=True, stop=True))
        P.phase()
        P.v(lambda v, rs_col=rs_col, s_col=s_col: v.tensor_copy(rs_col,
                                                                ctx.pb[2][:, 0:1]))
        P.v(lambda v, rs_col=rs_col, s_col=s_col: v.tensor_copy(s_col,
                                                                ctx.pb[2][:, 1:2]))
        P.phase()

    iscr = arena[:, 6144:8192].bitcast(I32)

    def qpass(dst, src, rs_col, s_col, W):
        P.v(lambda v: v.tensor_scalar(iscr[:, 0:W], src, rs_col, None, ALU.mult))
        P.phase()
        P.v(lambda v: v.tensor_scalar(dst, iscr[:, 0:W], s_col, None, ALU.mult))
        P.phase()

    for n in ("up", "gate", "down"):
        OT = 24 if n != "down" else 8
        I = 1024 if n != "down" else 3072
        nb = I // BLK
        wtile = arena[:, 0:I]
        sc = arena[:, 3104:3104 + nb]
        rsc = arena[:, 3232:3232 + nb]
        tkw = {0: P.dma(wtile, wraw[n][0], ch="mmld")}
        for ot in range(OT):
            P.phase()
            for e in ("vector", "scalar", "tensor"):
                P.wait(e, tkw[ot])
            P.phase()
            P.v(lambda v, I=I, nb=nb, wtile=wtile, sc=sc: v.tensor_reduce(
                sc, wtile.rearrange("p (b c) -> p b c", c=BLK), AX.X, ALU.max,
                apply_absolute_value=True))
            P.phase()
            P.v(lambda v, nb=nb, sc=sc: v.tensor_scalar(sc, sc, 1.0 / 7.0, 1e-12,
                                                        ALU.mult, ALU.max))
            P.phase()
            P.v(lambda v, nb=nb, sc=sc, rsc=rsc: v.reciprocal(rsc, sc))
            P.phase()
            for c0 in range(0, I, 1024):
                prod = arena[:, 3360:4384]
                deq = arena[:, 4384:4896].bitcast(BF16)
                bb = c0 // BLK
                P.v(lambda v, c0=c0, bb=bb, wtile=wtile, rsc=rsc:
                    v.tensor_tensor(prod.rearrange("p (b c) -> p b c", c=BLK),
                                    wtile[:, c0:c0 + 1024]
                                    .rearrange("p (b c) -> p b c", c=BLK),
                                    rsc[:, bb:bb + 32].unsqueeze(2)
                                    .to_broadcast([128, 32, BLK]), ALU.mult))
                P.phase()
                P.v(lambda v: v.tensor_copy(iscr[:, 0:1024], prod))
                P.phase()
                P.v(lambda v, bb=bb, sc=sc: v.tensor_tensor(
                    deq.rearrange("p (b c) -> p b c", c=BLK),
                    iscr[:, 0:1024].rearrange("p (b c) -> p b c", c=BLK),
                    sc[:, bb:bb + 32].unsqueeze(2).to_broadcast([128, 32, BLK]),
                    ALU.mult))
                P.phase()
                if ot + 1 < OT and c0 + 1024 >= I:
                    tkw[ot + 1] = P.dma(wtile, wraw[n][ot + 1], ch="mmld")
                stks = {}
                for kb in range(8):
                    P.t(lambda t, kb=kb, deq=deq: t.transpose(
                        ctx.pb[4 + kb % 2][:].bitcast(BF16)[:, 0:128],
                        deq[:, kb * 128:(kb + 1) * 128], ctx.ident_b[:]))
                    P.phase()
                    if kb >= 2:
                        P.wait("vector", stks[kb - 2])
                    P.v(lambda v, kb=kb: v.tensor_copy(
                        stg[:, kb % 2, 0:128],
                        ctx.pb[4 + kb % 2][:].bitcast(BF16)[:, 0:128]))
                    P.phase()
                    stks[kb] = P.dma(wdq[n][ot, (c0 // 128) + kb],
                                     stg[:, kb % 2, 0:128], ch="mmst")
                P.phase()
                P.wait_all("vector", "mmst")
                P.phase()
        P.phase()
        P.wait_all("gpsimd", "mmst")
        P.wait_all("sync", "mmst")
        P.phase()


    _global_absmax(P, ctx, [x[:] for x in ctx.XN], srow, 0, cols)
    allred_max()
    scales_from_slot(0, cols[:, 0:1], cols[:, 1:2])
    for k in range(NDT):
        qpass(ctx.XN[k][:], ctx.XN[k][:], cols[:, 0:1], cols[:, 1:2], T)

    P.phase()
    P.v(lambda v: v.memset(cols[:, 2:4], 0.0))
    P.phase()
    stage_tk = {}
    for mi, (mm, dst_d, mxc) in enumerate((("up", up_d, 2), ("gate", gate_d, 3))):
        tk = {0: P.dma(wsl[0][:], wdq[mm][0].rearrange("k p c -> p k c"),
                       ch="w", queue="gpsimd", allow_slow_non_contiguous=True)}
        for ob in range(24):
            for ci in range(NTC):
                P.phase()
                if ci == 0:
                    P.wait("tensor", tk[ob])
                    if ob + 1 < 24:
                        tk[ob + 1] = P.dma(wsl[(ob + 1) % 2][:],
                                           wdq[mm][ob + 1]
                                           .rearrange("k p c -> p k c"),
                                           ch="w", queue="gpsimd",
                                           allow_slow_non_contiguous=True)
                pbank = ctx.pb[ci % 4]
                lo = ci * TC
                for k in range(NDT):
                    P.t(lambda t, pb=pbank, k=k, ob=ob, lo=lo, mm=mm:
                        t.matmul(pb[:], wsl[ob % 2][:, k, :],
                                 ctx.XN[k][:, lo:lo + TC], start=(k == 0),
                                 stop=(k == NDT - 1)))
                P.phase()
                sl = stg[:, ci % 2, :]
                key = (mi * 96 + ob * NTC + ci) % 2
                if (mi * 96 + ob * NTC + ci) >= 2:
                    P.wait("vector", stage_tk[key])
                P.v(lambda v, sl=sl, pb=pbank: v.tensor_copy(sl, pb[:]))
                P.v(lambda v, pb=pbank: v.reduce_max(cols[:, 4:5], pb[:],
                                                     axis=AX.X,
                                                     apply_absolute_value=True))
                P.phase()
                stage_tk[key] = P.dma(dst_d[ob * 128:(ob + 1) * 128, lo:lo + TC],
                                      sl, ch="mmst")
                P.v(lambda v, mxc=mxc: v.tensor_tensor(cols[:, mxc:mxc + 1],
                                                       cols[:, mxc:mxc + 1],
                                                       cols[:, 4:5], ALU.max))
    P.phase()
    P.wait_all("gpsimd", "mmst")
    P.phase()
    col_to_slot(cols[:, 2:3], 1)
    col_to_slot(cols[:, 3:4], 2)
    allred_max()
    scales_from_slot(1, cols[:, 4:5], cols[:, 5:6])
    scales_from_slot(2, cols[:, 6:7], cols[:, 7:8])

    upb = arena[:, 0:1024].bitcast(BF16)
    gtb = arena[:, 1024:2048].bitcast(BF16)
    scrf = arena[:, 2048:4096]
    g2b = arena[:, 4096:5120].bitcast(BF16)
    P.phase()
    P.v(lambda v: v.memset(cols[:, 8:9], 0.0))
    P.phase()
    for kt in range(24):
        tkg = P.dma(gtb[:, 0:T], gate_d[kt * 128:(kt + 1) * 128, :], ch="mmld",
                    queue="gpsimd")
        P.phase()
        P.wait("vector", tkg)
        P.phase()
        qpass(gtb[:, 0:T], gtb[:, 0:T], cols[:, 6:7], cols[:, 7:8], T)
        P.s(lambda s: s.activation(scrf[:], gtb[:, 0:T], AF.Sigmoid))
        P.phase()
        P.v(lambda v: v.tensor_scalar(iscr[:, 0:T], scrf[:], 65536.0, None,
                                      ALU.mult))
        P.phase()
        P.v(lambda v: v.tensor_scalar(scrf[:], iscr[:, 0:T], 65535.0,
                                      1.0 / 65536.0, ALU.min, ALU.mult))
        P.phase()
        P.v(lambda v: v.tensor_tensor(g2b[:, 0:T], gtb[:, 0:T], scrf[:], ALU.mult))
        P.phase()
        P.v(lambda v: v.reduce_max(cols[:, 9:10], g2b[:, 0:T], axis=AX.X,
                                   apply_absolute_value=True))
        P.phase()
        P.v(lambda v: v.tensor_tensor(cols[:, 8:9], cols[:, 8:9], cols[:, 9:10],
                                      ALU.max))
        tkst = P.dma(g2_d[kt * 128:(kt + 1) * 128, :], g2b[:, 0:T], ch="mmst")
        P.phase()
        P.wait("gpsimd", tkst)
        P.phase()
    col_to_slot(cols[:, 8:9], 3)
    allred_max()
    scales_from_slot(3, cols[:, 10:11], cols[:, 11:12])

    P.phase()
    P.v(lambda v: v.memset(cols[:, 12:13], 0.0))
    P.phase()
    for kt in range(24):
        tku = P.dma(upb[:, 0:T], up_d[kt * 128:(kt + 1) * 128, :], ch="mmld",
                    queue="gpsimd")
        tkg = P.dma(gtb[:, 0:T], g2_d[kt * 128:(kt + 1) * 128, :], ch="mmld",
                    queue="gpsimd")
        P.phase()
        P.wait("vector", tku)
        P.wait("vector", tkg)
        P.phase()
        qpass(upb[:, 0:T], upb[:, 0:T], cols[:, 4:5], cols[:, 5:6], T)
        qpass(gtb[:, 0:T], gtb[:, 0:T], cols[:, 10:11], cols[:, 11:12], T)
        P.v(lambda v: v.tensor_tensor(g2b[:, 0:T], upb[:, 0:T], gtb[:, 0:T],
                                      ALU.mult))
        P.phase()
        P.v(lambda v: v.reduce_max(cols[:, 9:10], g2b[:, 0:T], axis=AX.X,
                                   apply_absolute_value=True))
        P.phase()
        P.v(lambda v: v.tensor_tensor(cols[:, 12:13], cols[:, 12:13],
                                      cols[:, 9:10], ALU.max))
        tkst = P.dma(o_d[kt * 128:(kt + 1) * 128, :], g2b[:, 0:T], ch="mmst")
        P.phase()
        P.wait("gpsimd", tkst)
        P.phase()
    col_to_slot(cols[:, 12:13], 4)
    allred_max()
    scales_from_slot(4, cols[:, 13:14], cols[:, 14:15])

    och = ctx.pool("l1f_och", [128, 24, TC], BF16)
    tko = {(0, 0): P.dma(wsl24[0][:], wdq["down"][0].rearrange("k p c -> p k c"),
                         ch="w", queue="gpsimd", allow_slow_non_contiguous=True)}
    for ci in range(NTC):
        lo = ci * TC
        tks = [P.dma(och[:, k, :], o_d[k * 128:(k + 1) * 128, lo:lo + TC],
                     ch="mmld", queue="gpsimd") for k in range(24)]
        P.phase()
        P.wait("vector", tks[-1])
        P.phase()
        for k in range(24):
            qpass(och[:, k, :], och[:, k, :], cols[:, 13:14], cols[:, 14:15], TC)
        for ob in range(NDT):
            nxt = (ci, ob + 1) if ob + 1 < NDT else (ci + 1, 0)
            P.phase()
            P.wait("tensor", tko[(ci, ob)])
            if nxt[0] < NTC:
                tko[nxt] = P.dma(wsl24[(nxt[0] * NDT + nxt[1]) % 2][:],
                                 wdq["down"][nxt[1]].rearrange("k p c -> p k c"),
                                 ch="w", queue="gpsimd",
                                 allow_slow_non_contiguous=True)
            pbank = ctx.pb[ob % 4]
            for k in range(24):
                P.t(lambda t, pb=pbank, k=k, ci=ci, ob=ob:
                    t.matmul(pb[:], wsl24[(ci * NDT + ob) % 2][:, k, :],
                             och[:, k, :], start=(k == 0), stop=(k == 23)))
            P.phase()
            P.v(lambda v, ob=ob, lo=lo, pb=pbank:
                v.tensor_tensor(ctx.X[ob][:, lo:lo + TC], ctx.X[ob][:, lo:lo + TC],
                                pb[:], ALU.add))
    P.phase()


def _global_absmax(P, ctx, tiles, scratch_row, out_slot, cols_scr):
    n = len(tiles)
    P.phase()
    for i, tl in enumerate(tiles):
        P.v(lambda v, i=i, tl=tl: v.reduce_max(cols_scr[:, 16 + i:17 + i], tl,
                                               axis=AX.X,
                                               apply_absolute_value=True))
    P.phase()
    P.v(lambda v: v.reduce_max(cols_scr[:, 16 + n:17 + n],
                               cols_scr[:, 16:16 + n], axis=AX.X))
    P.phase()
    P.t(lambda t: t.matmul(ctx.pb[2][0:1, 0:128], cols_scr[:, 16 + n:17 + n],
                           ctx.identf[:], start=True, stop=True))
    P.phase()
    P.v(lambda v: v.tensor_copy(scratch_row[0:1, 16:144], ctx.pb[2][0:1, 0:128]))
    P.phase()
    P.v(lambda v: v.reduce_max(scratch_row[0:1, out_slot:out_slot + 1],
                               scratch_row[0:1, 16:144], axis=AX.X))
    P.phase()


def host_L1_ffn(inputs, tag="L1f"):
    wu = np.asarray(inputs["L1_ffn_w_up"])
    wg = np.asarray(inputs["L1_ffn_w_gate"])
    wd = np.asarray(inputs["L1_ffn_w_down"])
    return {tag + "_up": np.ascontiguousarray(wu.reshape(24, 128, 1024)),
            tag + "_gate": np.ascontiguousarray(wg.reshape(24, 128, 1024)),
            tag + "_down": np.ascontiguousarray(wd.reshape(8, 128, 3072))}



def build_L1_attn_b(P, ctx):
    tag = "L1ab"
    HQ, HK, hd = 16, 8, 64
    GOFF = 6.906768
    wq_dr = P.dram(tag + "_wq", [HQ, 128, NDT, 64], F32, kind="ExternalInput").ap()
    wk_dr = P.dram(tag + "_wk", [HK, 128, NDT, 64], F32, kind="ExternalInput").ap()
    wv_dr = P.dram(tag + "_wv", [128, NDT, 512], F32, kind="ExternalInput").ap()
    wg_dr = P.dram(tag + "_wg", [128, NDT, 8], F32, kind="ExternalInput").ap()
    wp_dr = P.dram(tag + "_wp", [128, NDT, D], F32, kind="ExternalInput").ap()
    lnw_dr = P.dram(tag + "_lnw", [64, 1], F32, kind="ExternalInput").ap()
    wqh = [ctx.pool("attn_wqh%d" % i, [128, NDT, 64], BF16) for i in range(2)]
    wkh = [ctx.pool("l1b_wkh%d" % i, [128, NDT, 64], BF16) for i in range(2)]
    wb = ctx.pool("attn_wbuf", [128, NDT, D], BF16)
    wv_sb = wb[:, :, 0:512]
    wg_sb = P.sb(tag + "_wg_sb", [128, NDT, 8], BF16)
    lnw_full = P.sb(tag + "_lnw_sb", [128, 1])
    lnw_sb = lnw_full[0:64, :]
    tk_wv = P.dma(wv_sb[:], wv_dr, ch="w", queue="gpsimd")
    tk_wg = P.dma(wg_sb[:], wg_dr, ch="w", queue="gpsimd")
    P.dma(lnw_sb, lnw_dr, ch="wsmall")
    qk_tk = {}
    qk_tk[("q", 0)] = P.dma(wqh[0][:], wq_dr[0], ch="w", queue="gpsimd")
    qk_tk[("k", 0)] = P.dma(wkh[0][:], wk_dr[0], ch="w", queue="gpsimd")

    arena_b = ctx.arena[:].bitcast(BF16)
    Vtok = [arena_b[:, ts * 512:(ts + 1) * 512] for ts in range(NTT)]
    sincos = ctx.arena[:, 4096:6144]
    cos_t = sincos[:, 0:1024]
    sin_t = sincos[:, 1024:2048]
    logG = ctx.arena[0:8, 6144:8192]
    band = ctx.pool("ln_rows", [128, 4096])
    gj_r = band[0:1, 0:T]
    e1_r = band[0:1, T:2 * T]
    yth = ctx.pool("attn_yth", [128, 2, T], BF16)
    YT = [yth[:, k, :] for k in range(2)]
    qkband = ctx.pool("attn_qkh", [64, 2 * T], BF16)
    QH = qkband[:, 0:T]
    KH = qkband[:, T:2 * T]
    qtok = ctx.pool("l1b_qtok", [128, NTT, 64], BF16)
    tmpc = ctx.pool("attn_tmp", [128, TC])
    scal = ctx.pool("l1b_scal", [128, 8])
    zrow = ctx.pool("l1b_zrow", [8, T])

    P.phase()
    for e in ("vector", "scalar", "tensor"):
        P.wait_all(e, "wsmall")
    P.phase()

    sc_dr = P.dram(tag + "_sincos", [128, 2048], F32, kind="ExternalInput").ap()
    tk_sc = P.dma(sincos, sc_dr, ch="wsmall")
    P.phase()
    for e in ("vector", "scalar", "tensor"):
        P.wait_all(e, "wsmall")
    P.phase()

    for ci in range(NTC):
        P.phase()
        if ci == 0:
            P.wait("tensor", tk_wg)
        for k in range(NDT):
            P.t(lambda t, ci=ci, k=k: t.matmul(
                ctx.pb[ci % 2][0:8, :], wg_sb[:, k, :],
                ctx.XN[k][:, ci * TC:(ci + 1) * TC], start=(k == 0),
                stop=(k == NDT - 1)))
        P.phase()
        P.v(lambda v, ci=ci: v.tensor_scalar_add(tmpc[0:8, 0:TC],
                                                 ctx.pb[ci % 2][0:8, :], GOFF))
        P.phase()
        P.s(lambda s: s.activation(tmpc[0:8, 0:TC], tmpc[0:8, 0:TC], AF.Exp,
                                   scale=-1.0))
        P.phase()
        P.s(lambda s: s.activation(tmpc[0:8, 0:TC], tmpc[0:8, 0:TC], AF.Ln,
                                   bias=1.0))
        P.phase()
        P.v(lambda v, ci=ci: v.tensor_scalar_mul(logG[:, ci * TC:(ci + 1) * TC],
                                                 tmpc[0:8, 0:TC], -1.0))
        P.phase()
    P.v(lambda v: v.memset(zrow[:], 0.0))
    P.phase()
    P.v(lambda v: v.tensor_tensor_scan(logG[:], zrow[:], logG[:], 0.0,
                                       ALU.add, ALU.add))
    P.phase()

    def evv(ts, lo, hi, psum, eng):
        P.v(lambda v: v.tensor_copy(Vtok[ts][:, lo:hi], psum[:, 0:hi - lo]))
    proj_tok(P, ctx, wv_sb, 512, D, ctx.XN, evv, waits=[tk_wv])
    P.phase()
    tk_wp = P.dma(wb[:], wp_dr, ch="w", queue="gpsimd")
    P.phase()

    def rope_to_fm(dst):
        rt = tmpc[:, 0:64]
        ro = tmpc[:, 64:128]
        P.phase()
        P.v(lambda v: v.memset(tmpc[:, 160:192], 0.0))
        P.phase()
        for ts in range(NTT):
            P.phase()
            P.v(lambda v, ts=ts: v.tensor_scalar_mul(rt[:, 0:64:2],
                                                     qtok[:, ts, 1:64:2], -1.0))
            P.v(lambda v, ts=ts: v.tensor_copy(rt[:, 1:64:2], qtok[:, ts, 0:64:2]))
            P.phase()
            P.v(lambda v, ts=ts: v.tensor_tensor(ro, rt,
                                                 sin_t[:, ts * 64:(ts + 1) * 64],
                                                 ALU.mult))
            P.phase()
            P.v(lambda v, ts=ts: v.tensor_tensor(rt, qtok[:, ts, :],
                                                 cos_t[:, ts * 64:(ts + 1) * 64],
                                                 ALU.mult))
            P.phase()
            P.v(lambda v: v.tensor_tensor(rt, rt, ro, ALU.add))
            P.phase()
            P.v(lambda v: v.tensor_copy(tmpc[:, 128:192].bitcast(BF16)[:, 0:64],
                                        rt))
            P.phase()
            P.t(lambda t, ts=ts: t.transpose(
                ctx.pb[4 + ts % 2][:].bitcast(BF16)[:, 0:128],
                tmpc[:, 128:192].bitcast(BF16)[:, 0:128], ctx.ident_b[:]))
            P.phase()
            P.v(lambda v, ts=ts, dst=dst: v.tensor_copy(
                dst[:, ts * 128:(ts + 1) * 128],
                ctx.pb[4 + ts % 2][0:64].bitcast(BF16)[:, 0:128]))
            P.phase()

    def proj_tok_head(wgt):
        pend = None
        for ts in range(NTT):
            pbank = ctx.pb[ts % 2]
            P.phase()
            for k in range(NDT):
                P.t(lambda t, pb=pbank, k=k, ts=ts, wgt=wgt:
                    t.matmul(pb[:, 0:64], ctx.XN[k][:, ts * 128:(ts + 1) * 128],
                             wgt[:, k, :], start=(k == 0), stop=(k == NDT - 1)))
            if pend is not None:
                tsp, pj = pend
                P.v(lambda v, tsp=tsp, pj=pj: v.tensor_copy(qtok[:, tsp, :],
                                                            pj[:, 0:64]))
            pend = (ts, pbank)
        P.phase()
        tsp, pj = pend
        P.v(lambda v: v.tensor_copy(qtok[:, tsp, :], pj[:, 0:64]))
        P.phase()

    def get_qk(h):
        j = h // 2
        if h + 1 < HQ:
            qk_tk[("q", h + 1)] = P.dma(wqh[(h + 1) % 2][:], wq_dr[h + 1],
                                        ch="w", queue="gpsimd")
        if h % 2 == 0 and j + 1 < HK:
            qk_tk[("k", j + 1)] = P.dma(wkh[(j + 1) % 2][:], wk_dr[j + 1],
                                        ch="w", queue="gpsimd")
        P.phase()
        P.wait("tensor", qk_tk[("q", h)])
        if h % 2 == 0:
            P.wait("tensor", qk_tk[("k", j)])
        P.phase()
        proj_tok_head(wqh[h % 2])
        rope_to_fm(QH)
        if h % 2 == 0:
            proj_tok_head(wkh[j % 2])
            rope_to_fm(KH)
        for ci in range(NTC):
            P.t(lambda t, ci=ci, j=j: t.matmul(
                ctx.pb[2][0:1, :], ctx.identf[0:8, j:j + 1],
                logG[:, ci * TC:(ci + 1) * TC], start=True, stop=True))
            P.phase()
            P.v(lambda v, ci=ci: v.tensor_copy(gj_r[:, ci * TC:(ci + 1) * TC],
                                               ctx.pb[2][0:1, :]))
            P.phase()
        P.v(lambda v: v.tensor_scalar_mul(scal[0:1, 0:1], gj_r[:, T - 1:T], -1.0))
        P.phase()
        P.s(lambda s: s.activation(e1_r[:], gj_r[:], AF.Exp, bias=scal[0:1, 0:1]))
        P.phase()
        return QH, KH

    def strip_fn(h, ks, clo, chi, stp, a, qlo):
        w = chi - clo
        P.t(lambda t: t.matmul(ctx.pb[2][:, 0:1], e1_r[:, ks * 128:(ks + 1) * 128],
                               ctx.ones_row_f[0:1, 0:1], start=True, stop=True))
        P.t(lambda t: t.matmul(ctx.pb[3][:, 0:w], ctx.ones_row_f[:],
                               e1_r[:, clo:chi], start=True, stop=True))
        P.phase()
        P.v(lambda v: v.reciprocal(scal[:, 1:2], ctx.pb[2][:, 0:1]))
        P.phase()
        P.s(lambda s: s.activation(tmpc[:, 0:w], stp[:, 0:w], AF.Square,
                                   scale=0.125))
        P.phase()
        P.v(lambda v: v.scalar_tensor_tensor(a[:, clo - qlo:chi - qlo],
                                             tmpc[:, 0:w], scal[:, 1:2],
                                             ctx.pb[3][:, 0:w], ALU.mult,
                                             ALU.mult))
        P.phase()
        vlo2 = max(clo, ks * 128)
        if clo < vlo2:
            P.v(lambda v: v.memset(a[:, clo - qlo:vlo2 - qlo], 0.0))
        dlo, dhi = max(clo, ks * 128), min(chi, ks * 128 + 128)
        if dlo < dhi:
            P.phase()
            P.v(lambda v: v.tensor_tensor(a[:, dlo - qlo:dhi - qlo],
                                          a[:, dlo - qlo:dhi - qlo],
                                          ctx.utri_b[:, dlo - ks * 128:dhi - ks * 128],
                                          ALU.mult))

    yhead = ctx.pool("l1b_yhead", [64, T], BF16)
    shiftI = ctx.pool("l1b_shiftI", [64, 128], BF16)
    P.phase()
    P.v(lambda v: v.tensor_single_scalar(shiftI[:], ctx.jmi[0:64, :], 64.0,
                                         ALU.is_equal))
    P.phase()

    def fin(h, qc):
        nb = ctx.pb[4 + qc]
        ynum = tmpc[0:64, 0:TC]
        rr = zrow[0:1, 2 * TC:3 * TC]
        P.phase()
        P.s(lambda s: s.copy(ynum, nb[0:64, :]))
        P.v(lambda v, rr=rr: v.tensor_scalar_add(rr[:, 0:TC], nb[64:65, :], 1e-6))
        P.phase()
        P.v(lambda v, rr=rr: v.reciprocal(rr[:, 0:TC], rr[:, 0:TC]))
        P.phase()
        P.t(lambda t, rr=rr: t.matmul(ctx.pb[2][0:64, :], ctx.ones_row_f[:, 0:64],
                                      rr[:, 0:TC], start=True, stop=True))
        P.phase()
        P.v(lambda v, qc=qc: v.tensor_tensor(yhead[:, qc * TC:(qc + 1) * TC],
                                             ynum, ctx.pb[2][0:64, :], ALU.mult))
        P.phase()

    def ln_head(td, pbase):
        for ci in range(NTC):
            P.t(lambda t, ci=ci: t.matmul(
                ctx.pb[ci][0:1, :], ctx.ones_col_b[0:64, :],
                yhead[:, ci * TC:(ci + 1) * TC], start=True, stop=True))
            P.s(lambda s, ci=ci: s.activation(
                tmpc[0:64, :], yhead[:, ci * TC:(ci + 1) * TC], AF.Square))
            P.phase()
            P.t(lambda t, ci=ci: t.matmul(
                ctx.pb[4 + ci][0:1, :], ctx.ones_col[0:64, :],
                tmpc[0:64, :], start=True, stop=True))
            P.phase()
        for ci in range(NTC):
            P.v(lambda v, ci=ci: v.tensor_scalar_mul(
                gj_r[:, ci * TC:(ci + 1) * TC], ctx.pb[ci][0:1, :], 1.0 / 64.0))
        P.phase()
        for ci in range(NTC):
            va = zrow[0:1, 0:TC]
            vb = zrow[0:1, TC:2 * TC]
            P.v(lambda v, ci=ci, va=va: v.tensor_scalar_mul(
                va, ctx.pb[4 + ci][0:1, :], 1.0 / 64.0))
            P.phase()
            P.v(lambda v, ci=ci, vb=vb: v.tensor_tensor(
                vb, gj_r[:, ci * TC:(ci + 1) * TC],
                gj_r[:, ci * TC:(ci + 1) * TC], ALU.mult))
            P.phase()
            P.v(lambda v, va=va, vb=vb: v.tensor_sub(va, va, vb))
            P.phase()
            P.v(lambda v, va=va: v.tensor_scalar_add(va, va, 1e-9))
            P.phase()
            P.s(lambda s, va=va: s.activation(va, va, AF.Sqrt))
            P.phase()
            P.v(lambda v, ci=ci, va=va: v.reciprocal(
                e1_r[:, ci * TC:(ci + 1) * TC], va))
            P.phase()
        ysl = yth[:, td, :]
        for ci in range(NTC):
            lo = ci * TC
            P.t(lambda t, lo=lo: t.matmul(ctx.pb[0][0:64, :],
                                          ctx.ones_row_f[:, 0:64],
                                          gj_r[:, lo:lo + TC], start=True,
                                          stop=True))
            P.t(lambda t, lo=lo: t.matmul(ctx.pb[1][0:64, :],
                                          ctx.ones_row_f[:, 0:64],
                                          e1_r[:, lo:lo + TC], start=True,
                                          stop=True))
            P.phase()
            P.v(lambda v, lo=lo: v.tensor_tensor(
                tmpc[0:64, :], yhead[:, lo:lo + TC], ctx.pb[0][0:64, :],
                ALU.subtract))
            P.phase()
            P.v(lambda v, lo=lo: v.tensor_tensor(tmpc[0:64, :], tmpc[0:64, :],
                                                 ctx.pb[1][0:64, :], ALU.mult))
            P.phase()
            P.v(lambda v, lo=lo: v.tensor_scalar(
                yhead[:, lo:lo + TC], tmpc[0:64, :], lnw_sb[:, 0:1], None,
                ALU.mult))
            P.phase()
            if pbase == 0:
                P.v(lambda v, lo=lo, ysl=ysl: v.tensor_copy(
                    ysl[0:64, lo:lo + TC], yhead[:, lo:lo + TC]))
            else:
                P.t(lambda t, lo=lo: t.matmul(
                    ctx.pb[2][:], shiftI[:], yhead[:, lo:lo + TC], start=True,
                    stop=True))
                P.phase()
                P.v(lambda v, lo=lo, ysl=ysl: v.tensor_copy(
                    ysl[64:128, lo:lo + TC], ctx.pb[2][64:128, :]))
            P.phase()

    def evo_half(half):
        def evo(ob, lo, hi, psum, eng):
            P.v(lambda v: v.tensor_tensor(ctx.X[ob][:, lo:hi], ctx.X[ob][:, lo:hi],
                                          psum[:, 0:hi - lo], ALU.add))
        return evo

    def after_head(h):
        ln_head((h % 4) // 2, 64 * (h % 2))
        if h % 4 == 3:
            q = h // 4
            proj_fm(P, ctx, wb, D, D, YT, evo_half(q),
                    waits=[tk_wp] if q == 0 else (), kt_list=[2 * q, 2 * q + 1])

    attn_core(P, ctx, HQ, get_qk, Vtok, YT, strip_fn, fin, causal=True,
              with_den=True, after_head=after_head,
              vcol=lambda h: (h // 2) * 64)


def host_L1_attn_b(inputs):
    wa = np.asarray(inputs["L1_attn_b_w_attn"])
    return {
        "L1ab_wq": host_wT2_heads(wa[:, 0:1024], H=16),
        "L1ab_wk": host_wT2_heads(wa[:, 1024:1536], H=8),
        "L1ab_wv": host_wT2(wa[:, 1536:2048]),
        "L1ab_wg": host_wT2(wa[:, 2048:2056]),
        "L1ab_wp": host_wT2(np.asarray(inputs["L1_attn_b_w_proj"])),
        "L1ab_lnw": np.ascontiguousarray(np.asarray(inputs["L1_attn_b_ln_w"])
                                         [:, None]),
        "L1ab_sincos": host_rope_tables(),
    }


def host_rope_tables():
    div = (2.0 * np.pi / (float(T) ** (np.arange(0, 64, 2, dtype=np.float32)
                                       / np.float32(64)))).astype(np.float32)
    ang = (np.arange(T, dtype=np.float32)[:, None] * div[None, :]).astype(np.float32)
    sin = np.repeat(np.sin(ang.astype(np.float64)), 2, axis=1).astype(np.float32)
    cos = np.repeat(np.cos(ang.astype(np.float64)), 2, axis=1).astype(np.float32)
    out = np.zeros((128, 2048), np.float32)
    for ts in range(NTT):
        out[:, ts * 64:(ts + 1) * 64] = cos[ts * 128:(ts + 1) * 128]
        out[:, 1024 + ts * 64:1024 + (ts + 1) * 64] = sin[ts * 128:(ts + 1) * 128]
    return out



_TTN_STEPS = T
_TTN_PHASE = 9
_TTN_LVL = 9
_TTN_DBG = False
TTN_N = 64.0 * 1024.0


def build_L0_attn_b(P, ctx):
    tag = "ttn"
    ns = _TTN_STEPS
    arena_b = ctx.arena[:].bitcast(BF16)
    band = ctx.pool("ln_rows", [128, 4096])
    rstd_r = band[0:1, T:2 * T]
    vvrow = band[0:1, 0:T]
    rowA = band[0:1, T:2 * T]

    wk_sb, tk_wk = load_wT(P, tag + "_wk", 64, D)
    wq_sb, tk_wq = load_wT(P, tag + "_wq", 64, D)
    wb = ctx.pool("attn_wbuf", [128, NDT, D], BF16)
    wv_dr = P.dram(tag + "_wv", [128, NDT, D], F32, kind="ExternalInput").ap()
    wo_dr = P.dram(tag + "_wo", [128, NDT, D], F32, kind="ExternalInput").ap()
    tk_wv = P.dma(wb[:], wv_dr, ch="w", queue="gpsimd")
    nw_sb, tk_nw = load_plain(P, tag + "_nw", [128, NDT], ch="wsmall")
    eta_sb, tk_eta = load_plain(P, tag + "_eta", [64, 1], ch="wsmall")

    Msb = P.sb(tag + "_M", [64, 1024])
    Mb = P.sb(tag + "_Mb", [64, 1024], BF16)
    Kf = P.sb(tag + "_kf", [64, T], BF16)
    Ktf = P.sb(tag + "_ktf", [64, T], BF16)
    Ktok = P.sb(tag + "_ktok", [128, NTT, 64], BF16)
    kbuf = P.sb(tag + "_kbuf", [128, 2, 64], BF16)
    rowB = P.sb(tag + "_rowB", [1, T])
    vbuf = P.sb(tag + "_vbuf", [128, T], BF16)
    cols = P.sb(tag + "_cols", [128, 128])
    colsv = P.sb(tag + "_colsv", [128, 16])
    colsvv = P.sb(tag + "_colsvv", [128, 16])
    colsE = P.sb(tag + "_colsE", [16, 128])
    dj = P.sb(tag + "_dj", [1, TC])
    dja = P.sb(tag + "_dja", [1, TC])
    vr_sb = P.sb(tag + "_vr", [1, 1024])
    qacc = P.sb(tag + "_qacc", [1, 8])
    chr_ = P.sb(tag + "_chr", [1, 8])
    ch2 = P.sb(tag + "_ch2", [1, 8])
    scalrow = P.sb(tag + "_scalr", [1, 4])
    scal_sb = P.sb(tag + "_scalc", [64, 4])
    m2n = P.sb(tag + "_m2n", [1, 1])
    epsv = P.sb(tag + "_eps", [1, 1])
    etac = P.sb(tag + "_etac", [64, 1])

    P.phase()
    for k in range(NDT):
        P.v(lambda v, k=k: v.scalar_tensor_tensor(
            ctx.X[k][:], ctx.XN[k][:], 1.0, ctx.X[k][:], ALU.mult, ALU.add))
    P.phase()
    r2 = ctx.arena[0:1, 0:T]
    P.v(lambda v: v.tensor_tensor(r2, rstd_r, rstd_r, ALU.mult))
    P.phase()
    P.v(lambda v: v.tensor_scalar(r2, r2, -1e-5, 1.0 + 1e-6, ALU.mult, ALU.add))
    P.v(lambda v: v.memset(qacc[:], 0.0))
    P.v(lambda v: v.memset(chr_[:], 0.0))
    P.v(lambda v: v.memset(ch2[:], 0.0))
    P.v(lambda v: v.memset(m2n[:], 0.0))
    P.v(lambda v: v.memset(epsv[:], 1e-5))
    P.v(lambda v: v.memset(Msb[:], 0.0))
    P.v(lambda v: v.memset(Mb[:], 0.0))
    P.v(lambda v: v.memset(scalrow[:], 0.0))
    P.phase()
    P.s(lambda s: s.activation(r2, r2, AF.Sqrt))
    P.phase()
    P.v(lambda v: v.reciprocal(r2, r2))
    P.wait_all("vector", "wsmall")
    P.phase()
    P.v(lambda v: v.tensor_scalar(etac[:], eta_sb[:], -1.0, None, ALU.mult))
    P.phase()
    P.v(lambda v: v.tensor_tensor(etac[:], etac[:], eta_sb[:], ALU.max))
    P.phase()
    P.v(lambda v: v.tensor_single_scalar(etac[:], etac[:], 1e-6, ALU.max))
    P.phase()
    for ci in range(NTC):
        lo = ci * TC
        P.t(lambda t, lo=lo, ci=ci: t.matmul(
            ctx.pb[4 + ci % 2][:], ctx.ones_row_f[:], r2[:, lo:lo + TC],
            start=True, stop=True))
        P.phase()
        for k in range(NDT):
            P.v(lambda v, k=k, lo=lo, ci=ci: v.scalar_tensor_tensor(
                ctx.XN[k][:, lo:lo + TC], ctx.XN[k][:, lo:lo + TC],
                nw_sb[:, k:k + 1], ctx.pb[4 + ci % 2][:], ALU.mult, ALU.mult))
        P.phase()

    if _TTN_PHASE < 2:
        return
    def proj_phi(wsb, dst, waits):
        tA = band[0:64, 2048:2560]
        tB = band[0:64, 2560:3072]
        for ci in range(NTC):
            lo = ci * TC
            P.phase()
            if ci == 0:
                for w in waits:
                    P.wait("tensor", w)
            for k in range(NDT):
                P.t(lambda t, k=k, lo=lo, ci=ci, wsb=wsb: t.matmul(
                    ctx.pb[ci % 2][0:64, :], wsb[:, k, :],
                    ctx.XN[k][:, lo:lo + TC], start=(k == 0),
                    stop=(k == NDT - 1)))
            P.phase()
            P.v(lambda v, ci=ci, tA=tA: v.tensor_single_scalar(
                tA, ctx.pb[ci % 2][0:64, :], 0.0, ALU.min))
            P.v(lambda v, ci=ci, tB=tB: v.tensor_single_scalar(
                tB, ctx.pb[ci % 2][0:64, :], 0.0, ALU.max))
            P.phase()
            P.s(lambda s, tA=tA: s.activation(tA, tA, AF.Exp))
            P.phase()
            P.v(lambda v, lo=lo, tA=tA, tB=tB, dst=dst: v.tensor_tensor(
                dst[:, lo:lo + TC], tA, tB, ALU.add))
            P.phase()

    proj_phi(wk_sb, Kf, [tk_wk])
    P.v(lambda v: v.tensor_scalar(Ktf[:], Kf[:], etac[:, 0:1], None, ALU.mult))
    P.phase()

    P.v(lambda v: v.memset(vbuf[64:128, 0:256], 0.0))
    P.phase()
    for tt in range(NTT):
        st = vbuf[:, (tt % 2) * 128:(tt % 2) * 128 + 128]
        P.v(lambda v, tt=tt, st=st: v.tensor_copy(
            st[0:64, :], Ktf[:, tt * 128:(tt + 1) * 128]))
        P.phase()
        P.t(lambda t, tt=tt, st=st: t.transpose(
            ctx.pb[4 + tt % 2][:].bitcast(BF16)[:, 0:128], st, ctx.ident_b[:]))
        P.phase()
        P.v(lambda v, tt=tt: v.tensor_copy(
            Ktok[:, tt, :], ctx.pb[4 + tt % 2][:].bitcast(BF16)[:, 0:64]))
        P.phase()

    for ci in range(NTC):
        lo = ci * TC
        P.t(lambda t, ci=ci, lo=lo: t.matmul(
            ctx.pb[ci][0:1, :], ctx.ones_col_b[0:64, :], Ktf[:, lo:lo + TC],
            start=True, stop=True))
        P.s(lambda s, lo=lo: s.activation(band[0:64, 0:512], Ktf[:, lo:lo + TC],
                                          AF.Square))
        P.phase()
        P.t(lambda t, ci=ci: t.matmul(
            ctx.pb[4 + ci % 2][0:1, :], ctx.ones_col[0:64, :],
            band[0:64, 0:512], start=True, stop=True))
        P.phase()
        P.v(lambda v, ci=ci, lo=lo: v.tensor_copy(rowA[:, lo:lo + TC],
                                                  ctx.pb[ci][0:1, :]))
        P.v(lambda v, ci=ci, lo=lo: v.tensor_copy(rowB[:, lo:lo + TC],
                                                  ctx.pb[4 + ci % 2][0:1, :]))
        P.phase()

    if _TTN_PHASE < 3:
        return
    for ob in range(NDT):
        for ci in range(NTC):
            lo = ci * TC
            P.phase()
            if ob == 0 and ci == 0:
                P.wait("tensor", tk_wv)
            for k in range(NDT):
                P.t(lambda t, k=k, lo=lo, ci=ci, ob=ob: t.matmul(
                    ctx.pb[ci % 2][:], wb[:, k, ob * 128:(ob + 1) * 128],
                    ctx.XN[k][:, lo:lo + TC], start=(k == 0),
                    stop=(k == NDT - 1)))
            P.phase()
            P.v(lambda v, ci=ci, lo=lo: v.tensor_copy(vbuf[:, lo:lo + TC],
                                                      ctx.pb[ci % 2][:]))
            P.phase()
        for tt in range(NTT):
            P.t(lambda t, tt=tt: t.transpose(
                ctx.pb[4 + tt % 2][:].bitcast(BF16)[:, 0:128],
                vbuf[:, tt * 128:(tt + 1) * 128], ctx.ident_b[:]))
            P.phase()
            P.v(lambda v, tt=tt, ob=ob: v.tensor_copy(
                arena_b[:, tt * 1024 + ob * 128:tt * 1024 + (ob + 1) * 128],
                ctx.pb[4 + tt % 2][:].bitcast(BF16)[:, 0:128]))
            P.phase()

    for tt in range(NTT):
        vt = arena_b[:, tt * 1024:(tt + 1) * 1024]
        P.v(lambda v, tt=tt, vt=vt: v.tensor_reduce(colsv[:, tt:tt + 1], vt,
                                                    AX.X, ALU.add))
        P.v(lambda v, tt=tt, vt=vt: v.scalar_tensor_tensor(
            vbuf[:, 0:1024], vt, 1.0, vt, ALU.mult, ALU.mult,
            accum_out=colsvv[:, tt:tt + 1]))
        P.phase()
    P.v(lambda v: v.memset(cols[:], 0.0))
    P.phase()
    for which, src in ((0, colsv), (1, colsvv)):
        P.v(lambda v, src=src: v.tensor_copy(cols[:, 0:16], src[:]))
        P.phase()
        P.t(lambda t: t.transpose(ctx.pb[6][:, 0:128], cols[:], ctx.identf[:]))
        P.phase()
        P.v(lambda v: v.tensor_copy(colsE[:], ctx.pb[6][0:16, 0:128]))
        P.phase()
        for tt in range(NTT):
            P.dma(vvrow[:, tt * 128:(tt + 1) * 128], colsE[tt:tt + 1, :],
                  ch="rr")
        P.phase()
        P.wait_all("vector", "rr")
        if which == 0:
            P.v(lambda v: v.scalar_tensor_tensor(rowA[:], rowA[:], 1.0 / TTN_N,
                                                 vvrow[:], ALU.mult, ALU.mult))
        else:
            P.v(lambda v: v.scalar_tensor_tensor(rowB[:], rowB[:], 1.0 / TTN_N,
                                                 vvrow[:], ALU.mult, ALU.mult))
        P.phase()

    if _TTN_PHASE < 4:
        return
    for ci in range(NTC):
        lo = ci * TC
        P.v(lambda v, lo=lo: v.tensor_tensor(dj[:], rowA[:, lo:lo + TC],
                                             rowA[:, lo:lo + TC], ALU.mult))
        P.phase()
        P.v(lambda v, lo=lo: v.tensor_sub(rowB[:, lo:lo + TC],
                                          rowB[:, lo:lo + TC], dj[:]))
        P.phase()

    tk_wo = P.dma(wb[:], wo_dr, ch="w", queue="gpsimd")

    sp = P.sem("ttn_sp")
    sv = P.sem("ttn_sv")
    ss = P.sem("ttn_ss")
    sg = P.sem("ttn_sg")
    P.phase()
    for i in range(ns):
        tile, tt = divmod(i, 128)
        vlo = arena_b[:, tile * 1024:tile * 1024 + 512]
        vhi = arena_b[:, tile * 1024 + 512:tile * 1024 + 1024]
        if i > 0 and _TTN_LVL >= 5:
            P.t(lambda t, n=i: t.wait_ge(sg, n))
        P.t(lambda t, i=i: t.matmul(ctx.pb[0][0:1, :], Kf[:, i:i + 1],
                                    Mb[:, 0:512], start=True, stop=True))
        P.t(lambda t, i=i: t.matmul(ctx.pb[1][0:1, :], Kf[:, i:i + 1],
                                    Mb[:, 512:1024], start=True, stop=True))
        P.t(lambda t, i=i: t.matmul(ctx.pb[4][0:1, :], Ktf[:, i:i + 1],
                                    Mb[:, 0:512], start=True, stop=True))
        P.t(lambda t, i=i: t.matmul(ctx.pb[5][0:1, :], Ktf[:, i:i + 1],
                                    Mb[:, 512:1024], start=True, stop=True))
        if i > 0 and _TTN_LVL >= 5:
            P.t(lambda t, n=5 * (i - 1) + 5: t.wait_ge(ss, n))
        P.t(lambda t, tt=tt, vlo=vlo: t.matmul(
            ctx.pb[2][0:1, :], ctx.ident_b[:, tt:tt + 1], vlo, start=True,
            stop=True))
        P.t(lambda t, tt=tt, vhi=vhi: t.matmul(
            ctx.pb[3][0:1, :], ctx.ident_b[:, tt:tt + 1], vhi, start=True,
            stop=True).then_inc(sp, 1))
        if _TTN_LVL >= 5:
            P.t(lambda t, n=4 * i + 1: t.wait_ge(sv, n))
            P.t(lambda t, i=i, vlo=vlo: t.matmul(
                ctx.pb[6][0:64, :], kbuf[:, i % 2, :], vlo, start=True,
                stop=True))
            P.t(lambda t, i=i, vhi=vhi: t.matmul(
                ctx.pb[7][0:64, :], kbuf[:, i % 2, :], vhi, start=True,
                stop=True).then_inc(sp, 1))
            P.t(lambda t, n=4 * i + 3: t.wait_ge(sv, n))
            P.t(lambda t: t.matmul(ctx.pb[2][0:64, 0:3],
                                   ctx.ones_row_f[0:1, 0:64],
                                   scalrow[0:1, 0:3], start=True,
                                   stop=True).then_inc(sp, 1))
        def nopv():
            P.v(lambda v: v.memset(dj[0:1, 0:1], 0.0))
            P.v(lambda v: v.memset(dj[0:1, 2:3], 0.0))
            P.v(lambda v: v.memset(dj[0:1, 4:5], 0.0))
        if _TTN_LVL >= 3:
            P.v(lambda v, n=3 * i + 1: v.wait_ge(sp, n))
            P.v(lambda v, n=5 * i + 2: v.wait_ge(ss, n))
            P.v(lambda v: v.scalar_tensor_tensor(
                dj[:], ctx.pb[0][0:1, :], -2.0, vr_sb[0:1, 0:512],
                ALU.mult, ALU.mult, accum_out=qacc[0:1, 0:1]))
            P.v(lambda v: v.scalar_tensor_tensor(
                dj[:], ctx.pb[1][0:1, :], -2.0, vr_sb[0:1, 512:1024],
                ALU.mult, ALU.mult, accum_out=qacc[0:1, 1:2]))
            P.v(lambda v: v.scalar_tensor_tensor(
                dj[:], ctx.pb[4][0:1, :], 2.0 / TTN_N, vr_sb[0:1, 0:512],
                ALU.mult, ALU.mult, accum_out=qacc[0:1, 2:3]))
            P.v(lambda v: v.scalar_tensor_tensor(
                dj[:], ctx.pb[5][0:1, :], 2.0 / TTN_N, vr_sb[0:1, 512:1024],
                ALU.mult, ALU.mult, accum_out=qacc[0:1, 3:4]))
            P.v(lambda v: v.tensor_tensor(chr_[0:1, 0:1], qacc[0:1, 0:1],
                                          qacc[0:1, 1:2], ALU.add))
            P.v(lambda v: v.tensor_tensor(chr_[0:1, 1:2], qacc[0:1, 4:5],
                                          qacc[0:1, 5:6], ALU.add))
            P.v(lambda v: v.tensor_tensor(chr_[0:1, 2:3], qacc[0:1, 2:3],
                                          qacc[0:1, 3:4], ALU.add))
            if _TTN_LVL >= 5:
                P.v(lambda v, i=i, tile=tile, tt=tt: v.tensor_tensor(
                    kbuf[:, i % 2, :], Ktok[:, tile, :],
                    ctx.ident_b[:, tt:tt + 1].to_broadcast([128, 64]),
                    ALU.mult))
            else:
                nopv()
            P.v(lambda v, i=i: v.scalar_tensor_tensor(
                chr_[0:1, 3:4], chr_[0:1, 0:1], vvrow[0:1, i:i + 1],
                chr_[0:1, 1:2], ALU.add, ALU.add))
            nopv()
            nopv()
            if _TTN_DBG:
                P.v(lambda v: v.tensor_copy(qacc[0:1, 6:7], chr_[0:1, 3:4]))
            P.v(lambda v: v.tensor_single_scalar(chr_[0:1, 3:4], chr_[0:1, 3:4],
                                                 0.0, ALU.max).then_inc(sv, 1))
        if _TTN_LVL >= 4:
            P.v(lambda v, n=5 * i + 3: v.wait_ge(ss, n))
            P.v(lambda v: v.tensor_tensor(ch2[0:1, 1:2], scalrow[0:1, 0:1],
                                          chr_[0:1, 2:3], ALU.mult))
            P.v(lambda v, i=i: v.tensor_scalar(ch2[0:1, 4:5], scalrow[0:1, 0:1],
                                               rowA[0:1, i:i + 1], None,
                                               ALU.mult))
            if _TTN_DBG:
                P.v(lambda v: v.tensor_copy(qacc[0:1, 7:8], chr_[0:1, 3:4]))
            nopv()
            P.v(lambda v, i=i: v.scalar_tensor_tensor(
                ch2[0:1, 2:3], chr_[0:1, 3:4], rowB[0:1, i:i + 1], ch2[0:1, 1:2],
                ALU.mult, ALU.add))
            nopv()
            nopv()
            P.v(lambda v: v.tensor_tensor(ch2[0:1, 6:7], ch2[0:1, 2:3],
                                          m2n[0:1, 0:1],
                                          ALU.add).then_inc(sv, 1))
            P.v(lambda v, n=5 * i + 4: v.wait_ge(ss, n))
            P.v(lambda v: v.reciprocal(scalrow[0:1, 1:2], ch2[0:1, 7:8]))
            nopv()
            nopv()
            P.v(lambda v: v.tensor_tensor(chr_[0:1, 4:5], scalrow[0:1, 1:2],
                                          scalrow[0:1, 1:2], ALU.mult))
            P.v(lambda v: v.scalar_tensor_tensor(
                scalrow[0:1, 2:3], scalrow[0:1, 1:2], -1.0, ch2[0:1, 4:5],
                ALU.mult, ALU.mult).then_inc(sv, 1))
            nopv()
            P.v(lambda v: v.tensor_tensor(m2n[:], ch2[0:1, 6:7], chr_[0:1, 4:5],
                                          ALU.mult))
        if _TTN_LVL >= 5:
            P.v(lambda v, n=5 * i + 5: v.wait_ge(ss, n))
            P.v(lambda v: v.scalar_tensor_tensor(
                Msb[:, 0:512], ctx.pb[6][0:64, :], scal_sb[:, 0:1], Msb[:, 0:512],
                ALU.mult, ALU.add))
            P.v(lambda v: v.scalar_tensor_tensor(
                Msb[:, 512:1024], ctx.pb[7][0:64, :], scal_sb[:, 0:1],
                Msb[:, 512:1024], ALU.mult, ALU.add))
            nopv()
            nopv()
            P.v(lambda v: v.tensor_scalar(Msb[:], Msb[:], scal_sb[:, 1:2],
                                          scal_sb[:, 2:3], ALU.mult,
                                          ALU.add).then_inc(sv, 1))
        if _TTN_LVL >= 2:
            P.s(lambda s, n=3 * i + 1: s.wait_ge(sp, n))
            P.s(lambda s: s.copy(vr_sb[0:1, 0:512], ctx.pb[2][0:1, :]))
            P.s(lambda s: s.copy(vr_sb[0:1, 512:1024],
                                 ctx.pb[3][0:1, :]).then_inc(ss, 1))
            P.s(lambda s: s.activation(dja[:], ctx.pb[0][0:1, :], AF.Square,
                                       accum_out=qacc[0:1, 4:5]))
            P.s(lambda s: s.activation(dja[:], ctx.pb[1][0:1, :], AF.Square,
                                       accum_out=qacc[0:1, 5:6]).then_inc(ss, 1))
        if _TTN_LVL >= 4:
            P.s(lambda s, n=4 * i + 1: s.wait_ge(sv, n))
            P.s(lambda s: s.activation(scalrow[0:1, 0:1], chr_[0:1, 3:4],
                                       AF.Sqrt).then_inc(ss, 1))
            P.s(lambda s, n=4 * i + 2: s.wait_ge(sv, n))
            P.s(lambda s: s.activation(ch2[0:1, 7:8], ch2[0:1, 6:7], AF.Sqrt,
                                       bias=epsv[0:1, 0:1]).then_inc(ss, 1))
        if _TTN_LVL >= 5:
            P.s(lambda s, n=3 * i + 3: s.wait_ge(sp, n))
            P.s(lambda s: s.copy(scal_sb[:, 0:3],
                                 ctx.pb[2][0:64, 0:3]).then_inc(ss, 1))
        if _TTN_LVL >= 5:
            P.g(lambda g, n=4 * i + 4: g.wait_ge(sv, n))
            P.g(lambda g: g.tensor_copy(Mb[:], Msb[:]).then_inc(sg, 1))
    if ns > 0:
        P.t(lambda t: t.wait_ge(sp, 1))
        if _TTN_LVL >= 3:
            P.v(lambda v: v.wait_ge(sv, 1))
        if _TTN_LVL >= 2:
            P.s(lambda s: s.wait_ge(ss, 1))
        if _TTN_LVL >= 5:
            P.g(lambda g: g.wait_ge(sg, 1))
    P.phase()
    if _TTN_DBG:
        dbg = {"dbg_M": Msb[:], "dbg_Mb": Mb[:], "dbg_qacc": qacc[:],
               "dbg_chr": chr_[:], "dbg_ch2": ch2[:], "dbg_scal": scalrow[:],
               "dbg_m2n": m2n[:], "dbg_scalc": scal_sb[:],
               "dbg_vv": vvrow[0:1, 0:64], "dbg_rowA": rowA[0:1, 0:64],
               "dbg_rowB": rowB[0:1, 0:64], "dbg_kf": Kf[:, 0:64],
               "dbg_ktf": Ktf[:, 0:64], "dbg_vr": vr_sb[:]}
        for nm, ap in dbg.items():
            dr = P.dram(nm, [int(x) for x in ap.shape],
                        ap.dtype, kind="ExternalOutput").ap()
            P.dma(dr, ap, ch="dbgst")
        P.phase()
        P.wait_all("sync", "dbgst")
        P.phase()
    if _TTN_PHASE < 5:
        return

    def proj_phi2(wsb, dst, waits):
        tA = ctx.arena[0:64, 0:512]
        tB = ctx.arena[0:64, 512:1024]
        for ci in range(NTC):
            lo = ci * TC
            P.phase()
            if ci == 0:
                for w in waits:
                    P.wait("tensor", w)
            for k in range(NDT):
                P.t(lambda t, k=k, lo=lo, ci=ci, wsb=wsb: t.matmul(
                    ctx.pb[ci % 2][0:64, :], wsb[:, k, :],
                    ctx.XN[k][:, lo:lo + TC], start=(k == 0),
                    stop=(k == NDT - 1)))
            P.phase()
            P.v(lambda v, ci=ci, tA=tA: v.tensor_single_scalar(
                tA, ctx.pb[ci % 2][0:64, :], 0.0, ALU.min))
            P.v(lambda v, ci=ci, tB=tB: v.tensor_single_scalar(
                tB, ctx.pb[ci % 2][0:64, :], 0.0, ALU.max))
            P.phase()
            P.s(lambda s, tA=tA: s.activation(tA, tA, AF.Exp))
            P.phase()
            P.v(lambda v, lo=lo, tA=tA, tB=tB, dst=dst: v.tensor_tensor(
                dst[:, lo:lo + TC], tA, tB, ALU.add))
            P.phase()

    proj_phi2(wq_sb, Kf, [tk_wq])

    for dk in range(NDT):
        for ci in range(NTC):
            lo = ci * TC
            P.t(lambda t, dk=dk, ci=ci, lo=lo: t.matmul(
                ctx.pb[ci % 2][:], Mb[:, dk * 128:(dk + 1) * 128],
                Kf[:, lo:lo + TC], start=True, stop=True))
            P.phase()
            P.v(lambda v, dk=dk, ci=ci, lo=lo: v.tensor_copy(
                arena_b[:, dk * 2048 + lo:dk * 2048 + lo + TC],
                ctx.pb[ci % 2][:]))
            P.phase()

    for ob in range(NDT):
        for ci in range(NTC):
            lo = ci * TC
            P.phase()
            if ob == 0 and ci == 0:
                P.wait("tensor", tk_wo)
            for k in range(NDT):
                P.t(lambda t, k=k, lo=lo, ci=ci, ob=ob: t.matmul(
                    ctx.pb[ci % 2][:], wb[:, k, ob * 128:(ob + 1) * 128],
                    arena_b[:, k * 2048 + lo:k * 2048 + lo + TC],
                    start=(k == 0), stop=(k == NDT - 1)))
            P.phase()
            P.v(lambda v, ob=ob, lo=lo, ci=ci: v.tensor_tensor(
                ctx.X[ob][:, lo:lo + TC], ctx.X[ob][:, lo:lo + TC],
                ctx.pb[ci % 2][:], ALU.add))
            P.phase()


def host_L0_attn_b(inputs):
    return {
        "ttn_wk": host_wT(np.asarray(inputs["L0_attn_b_Wk"], dtype=np.float32)),
        "ttn_wq": host_wT(np.asarray(inputs["L0_attn_b_Wq"], dtype=np.float32)),
        "ttn_wv": host_wT(np.asarray(inputs["L0_attn_b_Wv"], dtype=np.float32)),
        "ttn_wo": host_wT(np.asarray(inputs["L0_attn_b_Wout"],
                                     dtype=np.float32)),
        "ttn_nw": np.ascontiguousarray(
            np.asarray(inputs["L0_attn_b_norm_weight"],
                       dtype=np.float32).reshape(NDT, 128).T),
        "ttn_eta": np.ascontiguousarray(
            np.asarray(inputs["L0_attn_b_eta"], dtype=np.float32)[:, None]),
    }



def load_X(P, ctx, xin):
    tks = [P.dma(ctx.X[k][:], xin[k * 128:(k + 1) * 128, :], ch="xload")
           for k in range(NDT)]
    P.phase()
    for e in ("tensor", "scalar", "vector", "gpsimd"):
        P.wait(e, tks[-1])
    P.phase()


def store_X(P, ctx, xout):
    P.phase()
    for k in range(NDT):
        P.dma(xout[k * 128:(k + 1) * 128, :], ctx.X[k][:], ch="xstore")
    P.phase()
    P.wait_all("sync", "xstore")
    P.phase()


def get_sublayers():
    return [
        ("L0_attn_a", build_L0_attn_a, host_L0_attn_a),
        ("L0_ffn", build_L0_ffn, host_L0_ffn),
        ("L0_attn_b", build_L0_attn_b, host_L0_attn_b),
        ("L0_ffn", build_L0_ffn, host_L0_ffn),
        ("L1_attn_a", build_L1_attn_a, host_L1_attn_a),
        ("L1_ffn", build_L1_ffn, host_L1_ffn),
        ("L1_attn_b", build_L1_attn_b, host_L1_attn_b),
        ("L1_ffn", build_L1_ffn, host_L1_ffn),
        ("L2_attn_a", build_L2_attn_a, host_L2_attn_a),
        ("L2_ffn", build_L2_ffn, host_L2_ffn),
        ("L2_attn_b", build_L2_attn_b, host_L2_attn_b),
        ("L2_ffn", build_L2_ffn, host_L2_ffn),
        ("L3_attn_a", build_L3_attn_a, host_L3_attn_a),
        ("L3_ffn", build_L3_ffn, host_L3_ffn),
        ("L3_attn_b", build_L3_attn_b, host_L3_attn_b, {"stats_only": True}),
        ("L3_ffn", build_L3_ffn, host_L3_ffn),
    ]


_PROG_CACHE = {}


def _build_single(tag, build_fn, stats_only=False):
    if tag in _PROG_CACHE:
        return _PROG_CACHE[tag]
    P = Prog()
    ctx = Ctx(P)
    xin = P.dram("x_in", [D, T], F32, kind="ExternalInput").ap()
    xout = P.dram("x_out", [D, T], F32, kind="ExternalOutput").ap()
    load_X(P, ctx, xin)
    layernorm_fm(P, ctx, ctx.X, None if stats_only else ctx.XN,
                 stats_only=stats_only)
    build_fn(P, ctx)
    store_X(P, ctx, xout)
    nc = P.finalize()
    _PROG_CACHE[tag] = nc
    return nc


SCRATCH_BYTES = 46336
SPILLABLE = ("L0_attn_a", "L3_attn_a", "L0_ffn", "L1_ffn", "L2_ffn", "L3_ffn")


def _build_fused():
    P = Prog()
    ctx = Ctx(P)
    xin = P.dram("x_in", [D, T], F32, kind="ExternalInput").ap()
    xout = P.dram("x_out", [D, T], F32, kind="ExternalOutput").ap()
    wbuf_base = (P.nc.sbuf_base + 31) & ~31
    ctx.pool("attn_wbuf", [128, NDT, D], BF16)
    P._spill_base = wbuf_base
    P._spill_size = NDT * D * 2
    P.scratch_setup(SCRATCH_BYTES)
    _ = ctx.XN
    load_X(P, ctx, xin)
    for entry in get_sublayers():
        opts = entry[3] if len(entry) > 3 else {}
        stats_only = opts.get("stats_only", False)
        ctx.new_frame()
        P.spill_ok = entry[0] in SPILLABLE
        P.scratch_mode = True
        layernorm_fm(P, ctx, ctx.X, None if stats_only else ctx.XN,
                     stats_only=stats_only)
        entry[1](P, ctx)
        P.scratch_mode = False
        P.spill_ok = False
    store_X(P, ctx, xout)
    return P.finalize()


def kernel(**inputs):
    x = np.asarray(inputs["L0_attn_a_hidden_states"], dtype=np.float32)
    xs = [np.ascontiguousarray(x[c].T) for c in range(N_CORES)]
    nc = _build_fused()
    host = {}
    for entry in get_sublayers():
        host.update(entry[2](inputs))
    in_maps = [dict(host, x_in=xs[c]) for c in range(N_CORES)]
    res = run_bass_kernel_spmd(nc, in_maps, core_ids=list(range(N_CORES)))
    out = np.stack([np.asarray(res.results[c]["x_out"]).T
                    for c in range(N_CORES)], axis=0)
    return np.ascontiguousarray(out)
